# version 03
# baseline (speedup 1.0000x reference)
import contextlib
import numpy as np
import ml_dtypes
import concourse.bass as bass
import concourse.mybir as mybir
from concourse.bass_utils import run_bass_kernel_spmd

F32 = mybir.dt.float32
BF16 = mybir.dt.bfloat16
BF = ml_dtypes.bfloat16
AF = mybir.ActivationFunctionType
ALU = mybir.AluOpType
AX = mybir.AxisListType

NCORE = 8
TOK = 2048
DM = 2048
HID = 8192
NS = 16
NHC = 64
NDC = 16
NQ = 4
QT = TOK // NQ
INV_D = 1.0 / DM
EPS = 1e-5
LADDER = (128.0, 64.0, 32.0, 16.0, 8.0, 4.0, 2.0, 1.0)
SERIAL_P2 = False


class Prog:
    ENGS = ("sync", "tensor", "vector", "scalar", "gpsimd")

    def __init__(self):
        self.ops = {e: [] for e in self.ENGS}
        self.cnt = {}
        self.sems = {}

    def w(self, eng, sname, val):
        if val <= 0:
            return
        self.ops[eng].append(lambda e, s=self.sems[sname], v=val: e.wait_ge(s, v))

    def do(self, eng, fn):
        self.ops[eng].append(fn)

    def dr(self, eng, fn):
        self.ops[eng].append(lambda e: e.drain())
        self.ops[eng].append(fn)

    def inc(self, eng, sname, fn, amt=1):
        self.cnt[sname] = self.cnt.get(sname, 0) + amt
        self.ops[eng].append(
            lambda e, f=fn, s=self.sems[sname], a=amt: f(e).then_inc(s, a))
        return self.cnt[sname]

    def val(self, sname):
        return self.cnt.get(sname, 0)


def build_program(debug=False):
    nc = bass.Bass("TRN2", target_bir_lowering=False)

    u_h = nc.dram_tensor("u", [TOK, DM], F32, kind="ExternalInput").ap()
    gw_h = nc.dram_tensor("gw", [HID, DM], F32, kind="ExternalInput").ap()
    vw_h = nc.dram_tensor("vw", [HID, DM], F32, kind="ExternalInput").ap()
    ow_h = nc.dram_tensor("ow", [DM, HID], F32, kind="ExternalInput").ap()
    gb_h = nc.dram_tensor("gbt", [128, NHC], F32, kind="ExternalInput").ap()
    vb_h = nc.dram_tensor("vbt", [128, NHC], F32, kind="ExternalInput").ap()
    ob_h = nc.dram_tensor("obt", [128, NDC], F32, kind="ExternalInput").ap()
    wbr_h = nc.dram_tensor("wbr", [DM, 64], F32, kind="ExternalInput").ap()
    wbs_h = nc.dram_tensor("wbs", [64, 256], F32, kind="ExternalInput").ap()
    wcr_h = nc.dram_tensor("wcr", [256, 64], F32, kind="ExternalInput").ap()
    wcm_h = nc.dram_tensor("wcm", [64, DM], F32, kind="ExternalInput").ap()
    a_h = nc.dram_tensor("afac", [1, 32], F32, kind="ExternalInput").ap()
    d_h = nc.dram_tensor("dvec", [1, DM], F32, kind="ExternalInput").ap()
    idb_h = nc.dram_tensor("idb", [128, 128], BF16, kind="ExternalInput").ap()
    idf_h = nc.dram_tensor("idf", [128, 128], F32, kind="ExternalInput").ap()
    out_h = nc.dram_tensor("out", [TOK, DM], F32, kind="ExternalOutput").ap()
    scr_kind = "ExternalOutput" if debug else "Internal"
    h_scr = nc.dram_tensor("h_scr", [NHC, 128, TOK], F32, kind=scr_kind).ap()
    x1_scr = nc.dram_tensor("x1_scr", [NS, 128, DM], F32, kind=scr_kind).ap()
    wqo_scr = nc.dram_tensor("wqo_scr", [NDC, 4, 128, DM], BF16, kind=scr_kind).ap()
    if debug:
        dbg_xi = nc.dram_tensor("dbg_xi", [128, NS * DM], BF16, kind="ExternalOutput").ap()
        dbg_cols = nc.dram_tensor("dbg_cols", [128, 24], F32, kind="ExternalOutput").ap()
        dbg_gx = nc.dram_tensor("dbg_gx", [128, NS], F32, kind="ExternalOutput").ap()
        dbg_g2 = nc.dram_tensor("dbg_g2", [128, NS], F32, kind="ExternalOutput").ap()
        dbg_g11 = nc.dram_tensor("dbg_g11", [1, 8], F32, kind="ExternalOutput").ap()
        dbg_sbc = nc.dram_tensor("dbg_sbc", [128, DM], F32, kind="ExternalOutput").ap()
        dbg_res1 = nc.dram_tensor("dbg_res1", [128, DM], F32, kind="ExternalOutput").ap()
        dbg_t1 = nc.dram_tensor("dbg_t1", [64, 128], BF16, kind="ExternalOutput").ap()
        dbg_t2 = nc.dram_tensor("dbg_t2", [64, 128], BF16, kind="ExternalOutput").ap()
        dbg_g = nc.dram_tensor("dbg_g", [64, 64], BF16, kind="ExternalOutput").ap()
        dbg_dp1 = nc.dram_tensor("dbg_dp1", [128, DM], F32, kind="ExternalOutput").ap()
        dbg_ubf = nc.dram_tensor("dbg_ubf", [128, DM], BF16, kind="ExternalOutput").ap()
        dbg_ufm = nc.dram_tensor("dbg_ufm", [128, DM], BF16, kind="ExternalOutput").ap()
        dbg_cols0 = nc.dram_tensor("dbg_cols0", [128, 24], F32, kind="ExternalOutput").ap()
        dbg_sg1 = nc.dram_tensor("dbg_sg1", [128, DM], F32, kind="ExternalOutput").ap()
        dbg_sg2 = nc.dram_tensor("dbg_sg2", [128, DM], F32, kind="ExternalOutput").ap()
        dbg_wq2o = nc.dram_tensor("dbg_wq2o", [128, DM], BF16, kind="ExternalOutput").ap()
        dbg_wqTo = nc.dram_tensor("dbg_wqTo", [128, DM], BF16, kind="ExternalOutput").ap()
        dbg_x1s0 = nc.dram_tensor("dbg_x1s0", [128, DM], F32, kind="ExternalOutput").ap()

    ctx = contextlib.ExitStack()
    sb = lambda n, s, d: ctx.enter_context(nc.sbuf_tensor(n, s, d))
    ps = lambda n, s, d: ctx.enter_context(nc.psum_tensor(n, s, d))

    AFW = 24600
    ABW = 52900
    A = sb("arena_f32", [128, AFW], F32)
    B = sb("arena_bf16", [128, ABW], BF16)

    def af(off, w, p0=0, p1=128):
        return A[p0:p1, off:off + w]

    def ab(off, w, p0=0, p1=128):
        return B[p0:p1, off:off + w]

    BUF0, BUF1 = 0, 2048
    W0, W1, W2, W3, W4, W5, W6, W7 = (4096 + 2048 * i for i in range(8))
    X0, X1 = 20480, 22528
    XI = 0
    RB0, RB1, RB2, RB3 = 32768, 34816, 36864, 38912
    SM = 40960
    WBRB, WCMB, WCRB, GB_, T1B, T2B, BTT = (SM, SM + 1024, SM + 3072, SM + 3200,
                                            SM + 3264, SM + 3392, SM + 3520)

    gb_sb = sb("gb_sb", [128, NHC], F32)
    vb_sb = sb("vb_sb", [128, NHC], F32)
    ob_sb = sb("ob_sb", [128, NDC], F32)
    gx_sb = sb("gx_sb", [128, NS], F32)
    g2_sb = sb("g2_sb", [128, NS], F32)
    r2_sb = sb("r2_sb", [128, NS], F32)
    s2_sb = sb("s2_sb", [128, NS], F32)
    row0 = sb("row0", [1, 128], F32)
    cols = sb("cols", [128, 24], F32)
    ones1 = sb("ones1", [1, 128], F32)
    onesc = sb("onesc", [128, 1], F32)
    g11 = sb("g11", [1, 8], F32)
    idb = sb("idb_s", [128, 128], BF16)
    idf = sb("idf_s", [128, 128], F32)
    (C_TGP, C_TGN, C_TVP, C_TVN, C_TOP, C_TON, C_SCG, C_SCV, C_SCO,
     C_T0, C_T1, C_T2, C_T3, C_RA, C_RT, C_SX, C_EPS) = range(17)

    p_g = ps("p_g", [128, 1024], F32)
    p_v = ps("p_v", [128, 1024], F32)
    p_tr = ps("p_tr", [128, 2048], BF16)
    p_t1 = ps("p_t1", [64, 512], F32)
    p_t2 = ps("p_t2", [64, 512], F32)

    P = Prog()
    semnames = ["PE", "DV", "AC", "GP", "L0", "LU0", "LU1", "LW0", "LW1",
                "LH0", "LH1", "LX0", "LX1", "LO0", "LO1",
                "SX0", "SX1", "SH0", "SH1", "SQ", "SO", "SD",
                "FD", "FG", "FA"]
    for nm in semnames:
        P.sems[nm] = ctx.enter_context(nc.semaphore("s" + nm))

    d_row = af(W0, DM, 0, 1)
    wbs_t = af(W1, 256, 0, 64)
    bb0 = af(W1 + 256, 256, 0, 64)
    bb1 = af(W1 + 512, 256, 0, 64)
    a_bc = af(W1 + 768, 32, 0, 64)
    a_row = af(W1 + 800, 32, 0, 1)
    tmp256 = af(W1 + 832, 256, 0, 64)
    wbr_f = af(W2, 1024)
    wcm_f = af(W3, DM, 0, 64)
    wcr_f = af(W4, 128)
    btT_f = af(W4 + 128, 128)
    dp1 = af(X0, DM)

    ld0 = [(gb_sb[:, :], gb_h[:, :]), (vb_sb[:, :], vb_h[:, :]),
           (ob_sb[:, :], ob_h[:, :]), (idb[:, :], idb_h[:, :]),
           (idf[:, :], idf_h[:, :]), (d_row, d_h[:, :]), (wbs_t, wbs_h[:, :]),
           (a_row, a_h[:, :]), (wcm_f, wcm_h[:, :])]
    for k in range(16):
        ld0.append((af(W2 + k * 64, 64), wbr_h[k * 128:(k + 1) * 128, :]))
    for k in range(2):
        ld0.append((af(W4 + k * 64, 64), wcr_h[k * 128:(k + 1) * 128, :]))
    for dst, src in ld0:
        P.inc("sync", "L0", lambda e, d=dst, s=src: e.dma_start(d, s), 16)
    ld0_done = P.val("L0")

    P.do("vector", lambda e: e.memset(cols[:, C_EPS:C_EPS + 1], float(EPS)))
    P.do("vector", lambda e: e.memset(ones1[:1, :], 1.0))
    P.do("vector", lambda e: e.memset(onesc[:, :], 1.0))
    P.w("vector", "L0", ld0_done)
    P.do("vector", lambda e: e.tensor_copy(ab(WBRB, 1024), wbr_f))
    P.do("vector", lambda e: e.tensor_copy(ab(WCMB, DM, 0, 64), wcm_f))
    P.do("vector", lambda e: e.tensor_copy(ab(WCRB, 128), wcr_f))
    dv_cast = P.inc("vector", "DV", lambda e: (e.drain(), e.engine_nop())[1])

    P.w("tensor", "DV", dv_cast)
    pe_abc = P.inc("tensor", "PE", lambda e: e.matmul(
        p_t1[0:64, 0:32], ones1[:1, 0:64], a_row, start=True, stop=True))
    P.w("vector", "PE", pe_abc)
    P.do("vector", lambda e: e.tensor_copy(a_bc, p_t1[0:64, 0:32]))
    P.do("vector", lambda e: e.tensor_copy(bb0, wbs_t))
    cur, nxt = bb0, bb1
    for lvl in range(8):
        wwid = 2 ** (7 - lvl)
        vi = cur.rearrange("p (b t w) -> p b t w", t=2, w=wwid)
        vo = nxt.rearrange("p (b t w) -> p b t w", t=2, w=wwid)
        vt = tmp256.rearrange("p (b t w) -> p b t w", t=2, w=wwid)
        for j in range(2):
            a0 = a_bc[:, lvl * 4 + j: lvl * 4 + j + 1]
            a1 = a_bc[:, lvl * 4 + 2 + j: lvl * 4 + 3 + j]
            P.do("vector", lambda e: e.drain())
            P.do("vector", lambda e, o=vt[:, :, j, :], i=vi[:, :, 0, :], s=a0:
                 e.tensor_single_scalar(o, i, s, ALU.mult))
            P.do("vector", lambda e: e.drain())
            P.do("vector", lambda e, o=vo[:, :, j, :], i=vi[:, :, 1, :], s=a1,
                 t=vt[:, :, j, :]:
                 e.scalar_tensor_tensor(o, i, s, t, ALU.mult, ALU.add))
        cur, nxt = nxt, cur
    dv_bfly = P.inc("vector", "DV", lambda e: (e.drain(), e.engine_nop())[1])

    P.w("tensor", "DV", dv_bfly)
    P.do("tensor", lambda e, i=cur: e.transpose(p_v[:, 0:64], i[:, 0:128],
                                                idf[0:64, 0:64]))
    pe_btt = P.inc("tensor", "PE", lambda e, i=cur: e.transpose(
        p_v[:, 64:128], i[:, 128:256], idf[0:64, 0:64]))
    P.w("vector", "PE", pe_btt)
    P.do("vector", lambda e: e.tensor_copy(btT_f, p_v[:, 0:128]))
    P.do("vector", lambda e: e.drain())
    dv_btt = P.inc("vector", "DV", lambda e: e.tensor_copy(ab(BTT, 128), btT_f))
    P.w("tensor", "DV", dv_btt)
    P.do("tensor", lambda e: e.matmul(p_t2[0:64, 0:64], ab(BTT, 64), ab(WCRB, 64),
                                      start=True, stop=False))
    pe_g = P.inc("tensor", "PE", lambda e: e.matmul(
        p_t2[0:64, 0:64], ab(BTT + 64, 64), ab(WCRB + 64, 64), start=False, stop=True))
    P.w("vector", "PE", pe_g)
    dv_g = P.inc("vector", "DV", lambda e: e.tensor_copy(ab(GB_, 64, 0, 64),
                                                         p_t2[0:64, 0:64]))
    P.w("tensor", "DV", dv_g)
    dv_dpe = {}
    for i in range(4):
        if i >= 2:
            P.w("tensor", "DV", dv_dpe[i - 2])
        pe_d = P.inc("tensor", "PE", lambda e, i=i: e.matmul(
            p_g[:, (i % 2) * 512:(i % 2) * 512 + 512], ones1[:1, :],
            d_row[:, i * 512:(i + 1) * 512], start=True, stop=True))
        P.w("vector", "PE", pe_d)
        dv_dpe[i] = P.inc("vector", "DV", lambda e, i=i: e.tensor_scalar(
            dp1[:, i * 512:(i + 1) * 512], p_g[:, (i % 2) * 512:(i % 2) * 512 + 512],
            1.0, 1.0, ALU.mult, ALU.add))
    dv_dp1 = P.inc("vector", "DV", lambda e: (e.drain(), e.engine_nop())[1])

    res1 = af(W0, DM)
    x1b = [af(W1, DM), af(W2, DM)]
    cS = af(W3, DM)
    bS = af(W4, DM)
    y2S = af(W5, DM)
    ubuf = [af(BUF0, DM), af(BUF1, DM)]
    u_bf = ab(RB0, DM)
    u_fm = ab(RB1, DM)
    xi_tm = ab(RB2, DM)
    xi_all = B[:, XI:XI + NS * DM].rearrange("p (dc t) -> p dc t", t=TOK)

    st_x1 = {}
    dv_strip_done = {}
    for s in range(NS):
        bp = s % 2
        if s >= 2:
            P.w("sync", "DV", dv_strip_done[s - 2])
        lu = P.inc("sync", "LU%d" % bp, lambda e, s=s, bp=bp: e.dma_start(
            ubuf[bp], u_h[s * 128:(s + 1) * 128, :]), 16)
        P.w("vector", "LU%d" % bp, lu)
        if s == 0:
            P.w("vector", "DV", dv_dp1)
        dv_ub = P.inc("vector", "DV", lambda e, bp=bp: e.tensor_copy(u_bf, ubuf[bp]))
        P.w("tensor", "DV", dv_ub)
        for k in range(16):
            fn = lambda e, k=k: e.transpose(
                p_tr[:, k * 128:(k + 1) * 128], u_bf[:, k * 128:(k + 1) * 128], idb[:, :])
            if k == 15:
                pe_utr = P.inc("tensor", "PE", fn)
            else:
                P.do("tensor", fn)
        P.w("vector", "PE", pe_utr)
        dv_ufm = P.inc("vector", "DV", lambda e: e.tensor_copy(u_fm, p_tr[:, :]))
        P.w("tensor", "DV", dv_ufm)
        for k in range(16):
            fn = lambda e, k=k: e.matmul(
                p_t1[0:64, 0:128], ab(WBRB + k * 64, 64), u_fm[:, k * 128:(k + 1) * 128],
                start=(k == 0), stop=(k == 15))
            if k == 15:
                pe_t1 = P.inc("tensor", "PE", fn)
            else:
                P.do("tensor", fn)
        P.w("vector", "PE", pe_t1)
        dv_t1 = P.inc("vector", "DV", lambda e: e.tensor_copy(
            ab(T1B, 128, 0, 64), p_t1[0:64, 0:128]))
        P.w("tensor", "DV", dv_t1)
        pe_t2 = P.inc("tensor", "PE", lambda e: e.matmul(
            p_t2[0:64, 0:128], ab(GB_, 64, 0, 64), ab(T1B, 128, 0, 64),
            start=True, stop=True))
        P.w("vector", "PE", pe_t2)
        dv_t2 = P.inc("vector", "DV", lambda e: e.tensor_copy(
            ab(T2B, 128, 0, 64), p_t2[0:64, 0:128]))
        P.w("tensor", "DV", dv_t2)
        for i in range(4):
            if i >= 2:
                P.w("tensor", "DV", P.val("DV"))
            pe_y = P.inc("tensor", "PE", lambda e, i=i: e.matmul(
                p_g[:, (i % 2) * 512:(i % 2) * 512 + 512], ab(T2B, 128, 0, 64),
                ab(WCMB + i * 512, 512, 0, 64), start=True, stop=True))
            P.w("vector", "PE", pe_y)
            sl = slice(i * 512, (i + 1) * 512)
            P.do("vector", lambda e, sl=sl, bp=bp: e.tensor_tensor(
                res1[:, sl], ubuf[bp][:, sl], dp1[:, sl], ALU.mult))
            P.do("vector", lambda e: e.drain())
            P.inc("vector", "DV", lambda e, i=i, sl=sl: e.tensor_tensor(
                res1[:, sl], res1[:, sl], p_g[:, (i % 2) * 512:(i % 2) * 512 + 512],
                ALU.add))
        if debug and s == 0:
            dv_dbg = P.val("DV")
            P.w("sync", "DV", dv_dbg)
            P.inc("sync", "SD", lambda e: e.dma_start(dbg_res1[:, :], res1), 16)
            P.inc("sync", "SD", lambda e: e.dma_start(dbg_t1[:, :], ab(T1B, 128, 0, 64)), 16)
            P.inc("sync", "SD", lambda e: e.dma_start(dbg_t2[:, :], ab(T2B, 128, 0, 64)), 16)
            P.inc("sync", "SD", lambda e: e.dma_start(dbg_g[:, :], ab(GB_, 64, 0, 64)), 16)
            P.inc("sync", "SD", lambda e: e.dma_start(dbg_dp1[:, :], dp1), 16)
            P.inc("sync", "SD", lambda e: e.dma_start(dbg_ubf[:, :], u_bf), 16)
            P.inc("sync", "SD", lambda e: e.dma_start(dbg_ufm[:, :], u_fm), 16)
            stdbg = P.val("SD")
            P.w("vector", "SD", stdbg)
        mcol = cols[:, C_T0:C_T0 + 1]
        vcol = cols[:, C_T1:C_T1 + 1]
        rcol = cols[:, C_T3:C_T3 + 1]
        P.do("vector", lambda e: e.drain())
        P.do("vector", lambda e: e.tensor_reduce(mcol, res1, AX.X, ALU.add))
        P.do("vector", lambda e: e.drain())
        P.do("vector", lambda e: e.tensor_single_scalar(mcol, mcol, INV_D, ALU.mult))
        P.do("vector", lambda e: e.drain())
        dv_ln = P.inc("vector", "DV", lambda e: e.tensor_single_scalar(
            res1, res1, mcol, ALU.subtract))
        lncol = cols[:, C_T2:C_T2 + 1]
        P.w("scalar", "DV", dv_ln)
        P.do("scalar", lambda e: e.activation(cS, res1, AF.Square,
                                              accum_out=vcol))
        P.do("scalar", lambda e: e.drain())
        P.do("scalar", lambda e, lncol=lncol: e.activation(
            lncol, vcol, AF.Ln, bias=cols[:, C_EPS:C_EPS + 1], scale=INV_D))
        P.do("scalar", lambda e: e.drain())
        ac_ln = P.inc("scalar", "AC", lambda e, lncol=lncol: e.activation(
            rcol, lncol, AF.Exp, scale=-0.5))
        xb = x1b[bp]
        P.w("vector", "AC", ac_ln)
        if s >= 2:
            P.w("vector", "SX%d" % bp, st_x1[s - 2])
        P.do("vector", lambda e: e.drain())
        dv_x1w = P.inc("vector", "DV", lambda e, xb=xb:
                       e.tensor_single_scalar(xb, res1, rcol, ALU.mult))
        P.w("sync", "DV", dv_x1w)
        st_x1[s] = P.inc("sync", "SX%d" % bp, lambda e, s=s, xb=xb: e.dma_start(
            x1_scr[s, :, :], xb), 16)
        if debug and s == 0:
            P.inc("sync", "SD", lambda e: e.dma_start(dbg_cols0[:, :], cols[:, :]), 16)
            P.inc("sync", "SD", lambda e, xb=xb: e.dma_start(dbg_x1s0[:, :], xb), 16)
            st0 = P.val("SD")
            P.w("vector", "SD", st0)
        gxc = gx_sb[:, s:s + 1]
        scol = cols[:, C_SX:C_SX + 1]
        P.do("vector", lambda e: e.drain())
        P.do("vector", lambda e, g=gxc, xb=xb: e.tensor_reduce(
            g, xb, AX.X, ALU.max, apply_absolute_value=True))
        P.do("vector", lambda e: e.drain())
        P.do("vector", lambda e, g=gxc: e.tensor_single_scalar(g, g, 1e-5, ALU.max))
        P.do("vector", lambda e: e.drain())
        P.do("vector", lambda e, g=gxc: e.tensor_single_scalar(
            cols[:, C_T2:C_T2 + 1], g, 1.0 / 127.0, ALU.mult))
        P.do("vector", lambda e: e.drain())
        P.do("vector", lambda e, sc=scol: e.reciprocal(sc, cols[:, C_T2:C_T2 + 1]))
        P.do("vector", lambda e: e.drain())
        P.dr("vector", lambda e, xb=xb, sc=scol: e.tensor_scalar(
            y2S, xb, sc, 128.5, ALU.mult, ALU.add))
        P.dr("vector", lambda e: e.tensor_copy(cS, y2S))
        for kk in LADDER:
            P.dr("vector", lambda e, kk=kk: e.tensor_scalar(
                bS, cS, kk, kk, ALU.is_ge, ALU.mult))
            P.dr("vector", lambda e: e.tensor_tensor(cS, cS, bS, ALU.subtract))
        P.dr("vector", lambda e: e.tensor_tensor(bS, y2S, cS, ALU.subtract))
        P.do("vector", lambda e: e.drain())
        dv_xi = P.inc("vector", "DV", lambda e: e.tensor_single_scalar(
            xi_tm, bS, 128.0, ALU.subtract))
        P.w("tensor", "DV", dv_xi)
        for k in range(16):
            fn = lambda e, k=k: e.transpose(
                p_tr[:, k * 128:(k + 1) * 128], xi_tm[:, k * 128:(k + 1) * 128], idb[:, :])
            if k == 15:
                pe_xtr = P.inc("tensor", "PE", fn)
            else:
                P.do("tensor", fn)
        P.w("vector", "PE", pe_xtr)
        ptr_view = p_tr[:, :].rearrange("p (dc t) -> p dc t", t=128)
        dv_strip_done[s] = P.inc("vector", "DV",
                                 lambda e, s=s, o=xi_all[:, :, s * 128:(s + 1) * 128],
                                 i=ptr_view: e.tensor_copy(o, i))

    wbuf = [af(BUF0, DM), af(BUF1, DM)]
    racc = cols[:, C_RA:C_RA + 1]
    rtmp2 = [cols[:, C_RT:C_RT + 1], cols[:, 17:18]]
    for mi in range(3):
        chunks = []
        if mi == 0:
            chunks = [gw_h[hc * 128:(hc + 1) * 128, :] for hc in range(64)]
        elif mi == 1:
            chunks = [vw_h[hc * 128:(hc + 1) * 128, :] for hc in range(64)]
        else:
            chunks = [ow_h[oc * 128:(oc + 1) * 128, s_ * 2048:(s_ + 1) * 2048]
                      for oc in range(16) for s_ in range(4)]
        P.do("vector", lambda e: e.memset(racc, 0.0))
        dv_red = {}
        for ci, src in enumerate(chunks):
            bp = ci % 2
            if mi == 0 and ci < 2:
                P.w("sync", "DV", dv_strip_done[14 + ci])
            elif ci < 2:
                P.w("sync", "DV", P.val("DV"))
            else:
                P.w("sync", "DV", dv_red[ci - 2])
            la = P.inc("sync", "LW%d" % bp, lambda e, d=wbuf[bp], s=src: e.dma_start(d, s), 16)
            P.w("vector", "LW%d" % bp, la)
            P.do("vector", lambda e, bp=bp, r=rtmp2[ci % 2]: e.tensor_reduce(
                r, wbuf[bp], AX.X, ALU.add, apply_absolute_value=True))
            P.do("vector", lambda e: e.drain())
            dv_red[ci] = P.inc("vector", "DV", lambda e, r=rtmp2[ci % 2]:
                               e.tensor_tensor(racc, racc, r, ALU.add))
        dv_ra = P.val("DV")
        P.w("tensor", "DV", dv_ra)
        pe_s = P.inc("tensor", "PE", lambda e, mi=mi: e.matmul(
            p_t1[0:1, 256 + mi:257 + mi], racc, onesc[:, :], start=True, stop=True))
        P.w("vector", "PE", pe_s)
        P.do("vector", lambda e, mi=mi: e.tensor_scalar(
            g11[:1, mi:mi + 1], p_t1[0:1, 256 + mi:257 + mi],
            1.0 / 16777216.0, 1e-5, ALU.mult, ALU.max))
        P.inc("vector", "DV", lambda e: (e.drain(), e.engine_nop())[1])
    dv_g11 = P.val("DV")
    P.w("tensor", "DV", dv_g11)
    pe_gb = P.inc("tensor", "PE", lambda e: e.matmul(
        p_v[:, 0:3], ones1[:1, :], g11[:1, 0:3], start=True, stop=True))
    P.w("vector", "PE", pe_gb)
    for gi, (cp, cn, csx) in enumerate([(C_TGP, C_TGN, C_SCG), (C_TVP, C_TVN, C_SCV),
                                        (C_TOP, C_TON, C_SCO)]):
        P.do("vector", lambda e, gi=gi, cp=cp: e.tensor_single_scalar(
            cols[:, cp:cp + 1], p_v[:, gi:gi + 1], 0.5, ALU.mult))
        P.do("vector", lambda e: e.drain())
        P.do("vector", lambda e, cp=cp, cn=cn: e.tensor_single_scalar(
            cols[:, cn:cn + 1], cols[:, cp:cp + 1], -1.0, ALU.mult))
        P.do("vector", lambda e, cp=cp, cs=csx: e.tensor_copy(
            cols[:, cs:cs + 1], cols[:, cp:cp + 1]))
        P.do("vector", lambda e: e.drain())
    s_bc = af(X1, DM)
    m_acc = af(X0, DM)
    P.do("vector", lambda e: e.memset(m_acc, 0.0))
    dv_pregx = P.inc("vector", "DV", lambda e: (e.drain(), e.engine_nop())[1])
    P.w("tensor", "DV", dv_pregx)
    for s in range(NS):
        pe_ct = P.inc("tensor", "PE", lambda e, s=s: e.transpose(
            p_v[0:1, 0:128], gx_sb[:, s:s + 1], idf[:, :]))
        P.w("vector", "PE", pe_ct)
        dv_r0 = P.inc("vector", "DV", lambda e: e.tensor_single_scalar(
            row0[:1, :], p_v[0:1, 0:128], 1.0 / 127.0, ALU.mult))
        P.w("tensor", "DV", dv_r0)
        pe_b = P.inc("tensor", "PE", lambda e: e.matmul(
            p_g[:, 0:128], ones1[:1, :], row0[:1, :], start=True, stop=True))
        P.w("vector", "PE", pe_b)
        P.inc("vector", "DV", lambda e, s=s: e.tensor_copy(
            s_bc[:, s * 128:(s + 1) * 128], p_g[:, 0:128]))

    if debug:
        P.w("sync", "DV", P.val("DV"))
        P.inc("sync", "SD", lambda e: e.dma_start(dbg_xi[:, :], B[:, XI:XI + NS * DM]), 16)
        P.inc("sync", "SD", lambda e: e.dma_start(dbg_sbc[:, :], s_bc), 16)
        P.inc("sync", "SD", lambda e: e.dma_start(dbg_cols[:, :], cols[:, :]), 16)
        P.inc("sync", "SD", lambda e: e.dma_start(dbg_gx[:, :], gx_sb[:, :]), 16)
        P.inc("sync", "SD", lambda e: e.dma_start(dbg_g11[:, :], g11[:1, :]), 16)

    g_t = af(W0, DM)
    v_t = af(W1, DM)
    gate_s = af(W2, DM)
    val_s = af(W3, DM)
    h_sb = [af(W4, DM), af(W5, DM)]
    sg1 = af(W6, DM)
    sg2 = af(W7, DM)
    wq2 = [ab(RB0, DM), ab(RB1, DM)]
    wqT = [ab(RB2, DM), ab(RB3, DM)]

    st_h = {}
    ac_sign_done = {}
    dv_wq_done = {}
    pe_tr_hist = {}
    P.w("vector", "SX0", st_x1[NS - 2])
    P.w("vector", "SX1", st_x1[NS - 1])
    P.w("scalar", "SX0", st_x1[NS - 2])
    P.w("scalar", "SX1", st_x1[NS - 1])
    for hc in range(NHC):
        ac_epi = 0
        for gate in (True, False):
            idx = hc * 2 + (0 if gate else 1)
            if SERIAL_P2:
                for _eng in ("sync", "tensor", "vector", "scalar"):
                    for _s in ("PE", "DV", "AC"):
                        P.w(_eng, _s, P.val(_s))
            bp = idx % 2
            wh = gw_h if gate else vw_h
            if idx >= 2:
                P.w("sync", "AC", ac_sign_done[idx - 2])
            else:
                P.w("sync", "DV", P.val("DV"))
            lw = P.inc("sync", "LW%d" % bp, lambda e, wh=wh, hc=hc, bp=bp: e.dma_start(
                wbuf[bp], wh[hc * 128:(hc + 1) * 128, :]), 16)
            P.w("scalar", "LW%d" % bp, lw)
            if idx >= 1:
                P.w("scalar", "DV", dv_wq_done[idx - 1])
            cn = C_TGN if gate else C_TVN
            cp = C_TGP if gate else C_TVP
            P.do("scalar", lambda e, bp=bp, cn=cn: e.activation(
                sg1, wbuf[bp], AF.Sign, bias=cols[:, cn:cn + 1]))
            ac_sign_done[idx] = P.inc("scalar", "AC", lambda e, bp=bp, cp=cp:
                                      e.activation(sg2, wbuf[bp], AF.Sign,
                                                   bias=cols[:, cp:cp + 1]))
            P.w("vector", "AC", ac_sign_done[idx])
            if idx >= 2:
                P.w("vector", "PE", pe_tr_hist[idx - 2])
            dv_wq_done[idx] = P.inc("vector", "DV", lambda e, bp=bp: e.tensor_tensor(
                wq2[bp], sg1, sg2, ALU.add))
            P.w("tensor", "DV", dv_wq_done[idx])
            for k in range(16):
                fn = lambda e, k=k, bp=bp: e.transpose(
                    p_tr[:, k * 128:(k + 1) * 128], wq2[bp][:, k * 128:(k + 1) * 128],
                    idb[:, :])
                if k == 15:
                    pe_tr = P.inc("tensor", "PE", fn)
                else:
                    P.do("tensor", fn)
            pe_tr_hist[idx] = pe_tr
            P.w("vector", "PE", pe_tr)
            dv_ev = P.inc("vector", "DV", lambda e, bp=bp: e.tensor_copy(
                wqT[bp], p_tr[:, :]))
            P.w("tensor", "DV", dv_ev)
            pp = p_g if gate else p_v
            tsc = g_t if gate else v_t
            for tq in range(4):
                if tq >= 2:
                    P.w("tensor", "DV", P.val("DV"))
                for k in range(16):
                    fn = lambda e, tq=tq, k=k, bp=bp, pp=pp: e.matmul(
                        pp[:, (tq % 2) * 512:(tq % 2) * 512 + 512],
                        wqT[bp][:, k * 128:(k + 1) * 128],
                        B[:, XI + k * TOK + tq * 512: XI + k * TOK + tq * 512 + 512],
                        start=(k == 0), stop=(k == 15))
                    if k == 15:
                        pe_tq = P.inc("tensor", "PE", fn)
                    else:
                        P.do("tensor", fn)
                P.w("vector", "PE", pe_tq)
                sl = slice(tq * 512, (tq + 1) * 512)
                P.inc("vector", "DV", lambda e, sl=sl, tq=tq, pp=pp, tsc=tsc:
                      e.tensor_tensor(tsc[:, sl],
                                      pp[:, (tq % 2) * 512:(tq % 2) * 512 + 512],
                                      s_bc[:, sl], ALU.mult))
            dv_epi4 = P.val("DV")
            P.w("scalar", "DV", dv_epi4)
            if gate:
                acfn = lambda e, hc=hc: e.activation(
                    gate_s, g_t, AF.Silu, bias=gb_sb[:, hc:hc + 1],
                    scale=cols[:, C_SCG:C_SCG + 1])
            else:
                acfn = lambda e, hc=hc: e.activation(
                    val_s, v_t, AF.Identity, bias=vb_sb[:, hc:hc + 1],
                    scale=cols[:, C_SCV:C_SCV + 1])
            ac_epi = P.inc("scalar", "AC", acfn)
        hp = hc % 2
        P.w("vector", "AC", ac_epi)
        if hc >= 2:
            P.w("vector", "SH%d" % hp, st_h[hc - 2])
        P.do("vector", lambda e, hp=hp: e.tensor_tensor(
            h_sb[hp], gate_s, val_s, ALU.mult))
        P.dr("vector", lambda e, hp=hp: e.tensor_single_scalar(
            g_t, h_sb[hp], -1.0, ALU.mult))
        P.do("vector", lambda e, hp=hp: e.tensor_tensor(
            m_acc, m_acc, h_sb[hp], ALU.max))
        P.do("vector", lambda e: e.drain())
        dv_h = P.inc("vector", "DV", lambda e: e.tensor_tensor(
            m_acc, m_acc, g_t, ALU.max))
        P.w("sync", "DV", dv_h)
        st_h[hc] = P.inc("sync", "SH%d" % hp, lambda e, hc=hc, hp=hp: e.dma_start(
            h_scr[hc, :, :], h_sb[hp]), 16)

    mt = af(W0, 128)
    for tb in range(16):
        dv_pre = P.inc("vector", "DV", lambda e: (e.drain(), e.engine_nop())[1])
        P.w("tensor", "DV", dv_pre)
        pe_mt = P.inc("tensor", "PE", lambda e, tb=tb: e.transpose(
            p_v[:, 0:128], m_acc[:, tb * 128:(tb + 1) * 128], idf[:, :]))
        P.w("vector", "PE", pe_mt)
        P.do("vector", lambda e: e.tensor_copy(mt, p_v[:, 0:128]))
        P.do("vector", lambda e: e.drain())
        P.do("vector", lambda e, tb=tb: e.tensor_reduce(
            g2_sb[:, tb:tb + 1], mt, AX.X, ALU.max))
        P.do("vector", lambda e: e.drain())
    P.do("vector", lambda e: e.drain())
    P.do("vector", lambda e: e.tensor_single_scalar(g2_sb[:, :], g2_sb[:, :],
                                                    1e-5, ALU.max))
    P.do("vector", lambda e: e.drain())
    P.do("vector", lambda e: e.reciprocal(r2_sb[:, :], g2_sb[:, :]))
    P.do("vector", lambda e: e.drain())
    P.do("vector", lambda e: e.tensor_single_scalar(r2_sb[:, :], r2_sb[:, :],
                                                    127.0, ALU.mult))
    dv_g2 = P.inc("vector", "DV", lambda e: e.tensor_single_scalar(
        s2_sb[:, :], g2_sb[:, :], 1.0 / 127.0, ALU.mult))

    if debug:
        P.w("sync", "DV", P.val("DV"))
        P.inc("sync", "SD", lambda e: e.dma_start(dbg_g2[:, :], g2_sb[:, :]), 16)

    wq2o = ab(RB0, DM)
    wqTo = ab(RB1, DM)
    ac_osign = {}
    dv_oq = {}
    for oc in range(NDC):
        for sub in range(4):
            j = oc * 4 + sub
            bp = j % 2
            if SERIAL_P2:
                for _eng in ("sync", "tensor", "vector", "scalar"):
                    for _s in ("PE", "DV", "AC"):
                        P.w(_eng, _s, P.val(_s))
            if j >= 2:
                P.w("sync", "AC", ac_osign[j - 2])
            else:
                P.w("sync", "AC", P.val("AC"))
            lo = P.inc("sync", "LW%d" % bp, lambda e, oc=oc, sub=sub, bp=bp: e.dma_start(
                wbuf[bp], ow_h[oc * 128:(oc + 1) * 128, sub * 2048:(sub + 1) * 2048]), 16)
            P.w("scalar", "LW%d" % bp, lo)
            if j >= 1:
                P.w("scalar", "DV", dv_oq[j - 1])
            P.do("scalar", lambda e, bp=bp: e.activation(
                sg1, wbuf[bp], AF.Sign, bias=cols[:, C_TON:C_TON + 1]))
            ac_osign[j] = P.inc("scalar", "AC", lambda e, bp=bp: e.activation(
                sg2, wbuf[bp], AF.Sign, bias=cols[:, C_TOP:C_TOP + 1]))
            if debug and j == 0:
                P.w("sync", "AC", ac_osign[j])
                P.inc("sync", "SD", lambda e: e.dma_start(dbg_sg1[:, :], sg1), 16)
                P.inc("sync", "SD", lambda e: e.dma_start(dbg_sg2[:, :], sg2), 16)
                _sd = P.val("SD")
                P.w("vector", "SD", _sd)
            P.w("vector", "AC", ac_osign[j])
            if j >= 1:
                P.w("vector", "PE", pe_q_prev)
            dv_oq[j] = P.inc("vector", "DV", lambda e: e.tensor_tensor(
                wq2o, sg1, sg2, ALU.add))
            if debug and j == 0:
                P.w("sync", "DV", dv_oq[j])
                P.inc("sync", "SD", lambda e: e.dma_start(dbg_wq2o[:, :], wq2o), 16)
                _sd = P.val("SD")
                P.w("tensor", "SD", _sd)
            P.w("tensor", "DV", dv_oq[j])
            for k in range(16):
                fn = lambda e, k=k: e.transpose(
                    p_tr[:, k * 128:(k + 1) * 128], wq2o[:, k * 128:(k + 1) * 128],
                    idb[:, :])
                if k == 15:
                    pe_q = P.inc("tensor", "PE", fn)
                else:
                    P.do("tensor", fn)
            pe_q_prev = pe_q
            P.w("vector", "PE", pe_q)
            if j >= 1:
                P.w("vector", "SQ", P.val("SQ"))
            dv_e = P.inc("vector", "DV", lambda e: e.tensor_copy(wqTo, p_tr[:, :]))
            P.w("sync", "DV", dv_e)
            if debug and j == 0:
                P.inc("sync", "SD", lambda e: e.dma_start(dbg_wqTo[:, :], wqTo), 16)
            P.inc("sync", "SQ", lambda e, oc=oc, sub=sub: e.dma_start(
                wqo_scr[oc, sub, :, :], wqTo), 16)

    zT4 = A[:, W0:W0 + 4 * DM].rearrange("p (tb r) -> p tb r", r=DM)
    x1rb = [af(W4, DM), af(W5, DM)]
    hbuf = [af(W6, 512), af(W6 + 512, 512)]
    r2q = af(W6 + 1024, 512)
    s2q = af(W6 + 1536, 512)
    cD = af(W7, 512)
    bD = af(W7 + 512, 512)
    y2D = af(W7 + 1024, 512)
    ovl = af(X0, 1024)
    ovq = af(X0 + 1024, 1024)
    sqW = af(W6, DM)
    cG = af(X1, 512)
    bG = af(X1 + 512, 512)
    y2G = af(X1 + 1024, 512)
    h_int = B[:, XI:XI + NHC * 512]
    wqoT = [ab(RB0, 8192), ab(RB0 + 8192, 8192)]

    dv_zadd = {}
    last_out_st = 0
    for q in range(NQ):
        P.w("tensor", "DV", P.val("DV"))
        for half, (dst, srccols) in enumerate([(r2q, r2_sb), (s2q, s2_sb)]):
            for sb4 in range(4):
                tb = q * 4 + sb4
                pe_ct = P.inc("tensor", "PE", lambda e, tb=tb, srccols=srccols:
                              e.transpose(p_v[0:1, 0:128], srccols[:, tb:tb + 1],
                                          idf[:, :]))
                P.w("vector", "PE", pe_ct)
                dv_r = P.inc("vector", "DV", lambda e: e.tensor_copy(
                    row0[:1, :], p_v[0:1, 0:128]))
                P.w("tensor", "DV", dv_r)
                pe_b = P.inc("tensor", "PE", lambda e, half=half: e.matmul(
                    p_g[:, (half % 2) * 512:(half % 2) * 512 + 128], ones1[:1, :],
                    row0[:1, :], start=True, stop=True))
                P.w("vector", "PE", pe_b)
                P.inc("vector", "DV", lambda e, dst=dst, sb4=sb4, half=half:
                      e.tensor_copy(dst[:, sb4 * 128:(sb4 + 1) * 128],
                                    p_g[:, (half % 2) * 512:(half % 2) * 512 + 128]))
        dv_bc_done = P.inc("vector", "DV", lambda e: (e.drain(), e.engine_nop())[1])
        P.w("gpsimd", "DV", dv_bc_done)

        if q == 0:
            P.w("sync", "SH0", st_h[NHC - 2])
            P.w("sync", "SH1", st_h[NHC - 1])
            P.w("sync", "SQ", P.val("SQ"))
            P.w("sync", "SX0", st_x1[NS - 2])
            P.w("sync", "SX1", st_x1[NS - 1])
        lad_dv = {}
        lad_gp = {}
        for jc in range(NHC):
            bp = jc % 2
            use_gp = (jc % 3 == 2)
            if jc >= 2:
                if (jc - 2) % 3 == 2:
                    P.w("sync", "GP", lad_gp[jc - 2])
                else:
                    P.w("sync", "DV", lad_dv[jc - 2])
            else:
                P.w("sync", "DV", P.val("DV"))
                P.w("sync", "GP", P.val("GP"))
            lh = P.inc("sync", "LH%d" % bp, lambda e, jc=jc, bp=bp, q=q: e.dma_start(
                hbuf[bp], h_scr[jc, :, q * QT:(q + 1) * QT]), 16)
            eng = "gpsimd" if use_gp else "vector"
            cc, bb_, yy = (cG, bG, y2G) if use_gp else (cD, bD, y2D)
            P.w(eng, "LH%d" % bp, lh)
            hi = h_int[:, jc * 512:(jc + 1) * 512]
            P.do(eng, lambda e, bp=bp, yy=yy: e.tensor_tensor(
                yy, hbuf[bp], r2q, ALU.mult))
            P.dr(eng, lambda e, yy=yy: e.tensor_scalar(
                yy, yy, 1.0, 128.5, ALU.mult, ALU.add))
            P.dr(eng, lambda e, cc=cc, yy=yy: e.tensor_copy(cc, yy))
            for kk in LADDER:
                P.dr(eng, lambda e, kk=kk, cc=cc, bb_=bb_: e.tensor_scalar(
                    bb_, cc, kk, kk, ALU.is_ge, ALU.mult))
                P.dr(eng, lambda e, cc=cc, bb_=bb_: e.tensor_tensor(
                    cc, cc, bb_, ALU.subtract))
            P.dr(eng, lambda e, bb_=bb_, yy=yy, cc=cc: e.tensor_tensor(
                bb_, yy, cc, ALU.subtract))
            P.do(eng, lambda e: e.drain())
            fin = lambda e, hi=hi, bb_=bb_: e.tensor_single_scalar(
                hi, bb_, 128.0, ALU.subtract)
            if use_gp:
                lad_gp[jc] = P.inc("gpsimd", "GP", fin)
            else:
                lad_dv[jc] = P.inc("vector", "DV", fin)
        lad_dv_all = P.val("DV")
        lad_gp_all = P.val("GP")

        dv_om = {}
        dv_zc = {}
        for dc in range(NDC):
            bp = dc % 2
            if dc >= 2 or q > 0:
                P.w("sync", "PE", P.val("PE"))
            lw = P.inc("sync", "LO%d" % bp, lambda e, dc=dc, bp=bp: e.dma_start(
                wqoT[bp].rearrange("p (s t) -> p s t", t=DM),
                wqo_scr[dc, :, :, :].rearrange("s p t -> p s t")), 16)
            P.w("tensor", "LO%d" % bp, lw)
            if dc == 0:
                P.w("tensor", "DV", lad_dv_all)
                P.w("tensor", "GP", lad_gp_all)
            if dc >= 2:
                P.w("tensor", "DV", dv_om[dc - 2])
            psl = p_g[:, (dc % 2) * 512:(dc % 2) * 512 + 512]
            for jc in range(NHC):
                fn = lambda e, jc=jc, bp=bp, psl=psl: e.matmul(
                    psl, wqoT[bp][:, jc * 128:(jc + 1) * 128],
                    h_int[:, jc * 512:(jc + 1) * 512],
                    start=(jc == 0), stop=(jc == NHC - 1))
                if jc == NHC - 1:
                    pe_o = P.inc("tensor", "PE", fn)
                else:
                    P.do("tensor", fn)
            P.w("vector", "PE", pe_o)
            ovp = ovl[:, (dc % 2) * 512:(dc % 2) * 512 + 512]
            dv_om[dc] = P.inc("vector", "DV", lambda e, psl=psl, ovp=ovp:
                              e.tensor_tensor(ovp, psl, s2q, ALU.mult))
            P.w("scalar", "DV", dv_om[dc])
            ovp2 = ovq[:, (dc % 2) * 512:(dc % 2) * 512 + 512]
            ac_ov = P.inc("scalar", "AC", lambda e, ovp=ovp, ovp2=ovp2, dc=dc:
                          e.activation(ovp2, ovp, AF.Identity,
                                       bias=ob_sb[:, dc:dc + 1],
                                       scale=cols[:, C_SCO:C_SCO + 1]))
            P.w("tensor", "AC", ac_ov)
            if dc >= 1:
                P.w("tensor", "DV", dv_zc[dc - 1])
            for tb4 in range(4):
                fn = lambda e, tb4=tb4, ovp=ovp2: e.transpose(
                    p_v[:, tb4 * 128:(tb4 + 1) * 128],
                    ovp[:, tb4 * 128:(tb4 + 1) * 128], idf[:, :])
                if tb4 == 3:
                    pe_ot = P.inc("tensor", "PE", fn)
                else:
                    P.do("tensor", fn)
            P.w("vector", "PE", pe_ot)
            if dc == 0 and q > 0:
                P.w("vector", "SO", last_out_st)
            pv_view = p_v[:, 0:512].rearrange("p (tb r) -> p tb r", r=128)
            zt_view = zT4[:, :, dc * 128:dc * 128 + 128]
            dv_zc[dc] = P.inc("vector", "DV", lambda e, i=pv_view, o=zt_view:
                              e.tensor_copy(o, i))

        for tb4 in range(4):
            s_idx = q * 4 + tb4
            bp = tb4 % 2
            if s_idx >= 2:
                P.w("sync", "DV", dv_zadd.get(s_idx - 2, 0))
            lx = P.inc("sync", "LX%d" % bp, lambda e, s_idx=s_idx, bp=bp: e.dma_start(
                x1rb[bp], x1_scr[s_idx, :, :]), 16)
            zt = af(W0 + tb4 * DM, DM)
            P.w("vector", "LX%d" % bp, lx)
            P.do("vector", lambda e: e.drain())
            dv_zadd[s_idx] = P.inc("vector", "DV", lambda e, zt=zt, bp=bp:
                                   e.tensor_tensor(zt, zt, x1rb[bp], ALU.add))
            P.do("vector", lambda e: e.drain())
            mcol = cols[:, C_T0:C_T0 + 1]
            vcol = cols[:, C_T1:C_T1 + 1]
            rcol = cols[:, C_T3:C_T3 + 1]
            P.do("vector", lambda e, zt=zt: e.tensor_reduce(mcol, zt, AX.X, ALU.add))
            P.do("vector", lambda e: e.drain())
            P.do("vector", lambda e: e.tensor_single_scalar(mcol, mcol, INV_D, ALU.mult))
            P.do("vector", lambda e: e.drain())
            dv_l2 = P.inc("vector", "DV", lambda e, zt=zt: e.tensor_single_scalar(
                zt, zt, mcol, ALU.subtract))
            lncol = cols[:, C_T2:C_T2 + 1]
            P.w("scalar", "DV", dv_l2)
            P.do("scalar", lambda e, zt=zt: e.activation(sqW, zt, AF.Square,
                                                         accum_out=vcol))
            P.do("scalar", lambda e: e.drain())
            P.do("scalar", lambda e, lncol=lncol: e.activation(
                lncol, vcol, AF.Ln, bias=cols[:, C_EPS:C_EPS + 1], scale=INV_D))
            P.do("scalar", lambda e: e.drain())
            ac_l2 = P.inc("scalar", "AC", lambda e, lncol=lncol: e.activation(
                rcol, lncol, AF.Exp, scale=-0.5))
            P.w("vector", "AC", ac_l2)
            P.do("vector", lambda e: e.drain())
            dv_x2 = P.inc("vector", "DV", lambda e, zt=zt: e.tensor_single_scalar(
                zt, zt, rcol, ALU.mult))
            P.w("sync", "DV", dv_x2)
            last_out_st = P.inc("sync", "SO", lambda e, s_idx=s_idx, zt=zt:
                                e.dma_start(out_h[s_idx * 128:(s_idx + 1) * 128, :],
                                            zt), 16)

    P.w("sync", "SO", P.val("SO"))
    if debug:
        P.w("sync", "SD", P.val("SD"))

    with nc.Block() as block:
        @block.sync
        def _(eng):
            for f in P.ops["sync"]:
                f(eng)

        @block.tensor
        def _(eng):
            for f in P.ops["tensor"]:
                f(eng)

        @block.vector
        def _(eng):
            for f in P.ops["vector"]:
                f(eng)

        @block.scalar
        def _(eng):
            for f in P.ops["scalar"]:
                f(eng)

        @block.gpsimd
        def _(eng):
            for f in P.ops["gpsimd"]:
                f(eng)

    ctx.close()
    return nc


_NC_CACHE = None


def kernel(L0_attn_u, L0_attn_A_factors, L0_attn_W_b_rank, L0_attn_W_b_state,
           L0_attn_W_c_rank, L0_attn_W_c_model, L0_attn_D_vec,
           L0_ffn_gate_w, L0_ffn_gate_b, L0_ffn_val_w, L0_ffn_val_b,
           L0_ffn_out_w, L0_ffn_out_b):
    global _NC_CACHE
    u = np.ascontiguousarray(np.asarray(L0_attn_u, dtype=np.float32).reshape(-1, DM))
    shared = dict(
        gw=np.ascontiguousarray(np.asarray(L0_ffn_gate_w, dtype=np.float32)),
        vw=np.ascontiguousarray(np.asarray(L0_ffn_val_w, dtype=np.float32)),
        ow=np.ascontiguousarray(np.asarray(L0_ffn_out_w, dtype=np.float32)),
        gbt=np.ascontiguousarray(
            np.asarray(L0_ffn_gate_b, dtype=np.float32).reshape(NHC, 128).T),
        vbt=np.ascontiguousarray(
            np.asarray(L0_ffn_val_b, dtype=np.float32).reshape(NHC, 128).T),
        obt=np.ascontiguousarray(
            np.asarray(L0_ffn_out_b, dtype=np.float32).reshape(NDC, 128).T),
        wbr=np.ascontiguousarray(np.asarray(L0_attn_W_b_rank, dtype=np.float32)),
        wbs=np.ascontiguousarray(np.asarray(L0_attn_W_b_state, dtype=np.float32)),
        wcr=np.ascontiguousarray(np.asarray(L0_attn_W_c_rank, dtype=np.float32)),
        wcm=np.ascontiguousarray(np.asarray(L0_attn_W_c_model, dtype=np.float32)),
        afac=np.ascontiguousarray(
            np.asarray(L0_attn_A_factors, dtype=np.float32).reshape(1, 32)),
        dvec=np.ascontiguousarray(
            np.asarray(L0_attn_D_vec, dtype=np.float32).reshape(1, DM)),
        idb=np.eye(128, dtype=np.float32).astype(BF),
        idf=np.eye(128, dtype=np.float32),
    )
    if _NC_CACHE is None:
        _NC_CACHE = build_program()
    in_maps = [dict(u=np.ascontiguousarray(u[c * TOK:(c + 1) * TOK, :]), **shared)
               for c in range(NCORE)]
    res = run_bass_kernel_spmd(_NC_CACHE, in_maps, core_ids=list(range(NCORE)))
    out = np.concatenate(
        [np.asarray(r["out"], dtype=np.float32) for r in res.results], axis=0)
    return out.reshape(4, 4096, DM)


# version 04
# speedup vs baseline: 2.1661x; 2.1661x over previous
import contextlib
import numpy as np
import ml_dtypes
import concourse.bass as bass
import concourse.mybir as mybir
from concourse.bass_utils import run_bass_kernel_spmd

F32 = mybir.dt.float32
BF16 = mybir.dt.bfloat16
BF = ml_dtypes.bfloat16
AF = mybir.ActivationFunctionType
ALU = mybir.AluOpType
AX = mybir.AxisListType

NCORE = 8
TOK = 2048
DM = 2048
HID = 8192
NS = 16
NHG = 16
NK = 16
NJC = 64
NQ = 4
INV_D = 1.0 / DM
EPS = 1e-5
LADK = (128.0, 64.0, 32.0, 16.0, 8.0, 4.0, 2.0, 1.0)

O_COLS = 0
O_DP1 = 128
O_UB = 2176
O_UT = 4224
O_RES = 6272
O_X1 = 8320
O_LC = 10368
O_LY = 12416
O_GS = 14464
O_WL = 22656
O_PQ = 26752
O_BR = 28800
A_W = 29824
B_XI = 0
B_GWA = 32768
B_SG = 40960
B_OQ = 43008
B_W = 45056

(C_EPS, C_HALF, C_T0, C_T1, C_T2, C_T3, C_SCG, C_SCV, C_SCO,
 C_ISG, C_ISV, C_RA0, C_RA1, C_RA2, C_SA0, C_SA1, C_SA2, C_SB0, C_SB1,
 C_SB2, C_K0) = range(21)
C_MAC = 32
C_SXS = 48
C_SGC = 64
C_SVC = 80
C_S2 = 96
C_GXI = 112
NCOL = 128


class Prog:
    ENGS = ("sync", "tensor", "vector", "scalar", "gpsimd")

    def __init__(self):
        self.ops = {e: [] for e in self.ENGS}
        self.cnt = {}
        self.sems = {}

    def w(self, eng, sname, val):
        if val <= 0:
            return
        self.ops[eng].append(lambda e, s=self.sems[sname], v=val: e.wait_ge(s, v))

    def do(self, eng, fn):
        self.ops[eng].append(fn)

    def inc(self, eng, sname, fn, amt=1):
        self.cnt[sname] = self.cnt.get(sname, 0) + amt
        self.ops[eng].append(
            lambda e, f=fn, s=self.sems[sname], a=amt: f(e).then_inc(s, a))
        return self.cnt[sname]

    def val(self, sname):
        return self.cnt.get(sname, 0)


def build_program(debug=False):
    nc = bass.Bass("TRN2", target_bir_lowering=False)

    u_h = nc.dram_tensor("u", [TOK, DM], F32, kind="ExternalInput").ap()
    ut_h = nc.dram_tensor("ut", [DM, TOK], F32, kind="ExternalInput").ap()
    gwt_h = nc.dram_tensor("gwt", [DM, HID], F32, kind="ExternalInput").ap()
    vwt_h = nc.dram_tensor("vwt", [DM, HID], F32, kind="ExternalInput").ap()
    owt_h = nc.dram_tensor("owt", [HID, DM], F32, kind="ExternalInput").ap()
    gbr_h = nc.dram_tensor("gbr", [1, HID], F32, kind="ExternalInput").ap()
    vbr_h = nc.dram_tensor("vbr", [1, HID], F32, kind="ExternalInput").ap()
    obr_h = nc.dram_tensor("obr", [1, DM], F32, kind="ExternalInput").ap()
    wbr_h = nc.dram_tensor("wbr", [DM, 64], F32, kind="ExternalInput").ap()
    wbs_h = nc.dram_tensor("wbs", [64, 256], F32, kind="ExternalInput").ap()
    wcr_h = nc.dram_tensor("wcr", [256, 64], F32, kind="ExternalInput").ap()
    wcm_h = nc.dram_tensor("wcm", [64, DM], F32, kind="ExternalInput").ap()
    a_h = nc.dram_tensor("afac", [1, 32], F32, kind="ExternalInput").ap()
    d_h = nc.dram_tensor("dvec", [1, DM], F32, kind="ExternalInput").ap()
    idf_h = nc.dram_tensor("idf", [128, 128], F32, kind="ExternalInput").ap()
    idb_h = nc.dram_tensor("idb", [128, 128], BF16, kind="ExternalInput").ap()
    out_h = nc.dram_tensor("out", [TOK, DM], F32, kind="ExternalOutput").ap()
    scr_kind = "ExternalOutput" if debug else "Internal"
    x1_scr = nc.dram_tensor("x1_scr", [NS, 128, DM], F32, kind=scr_kind).ap()
    h_scr = nc.dram_tensor("h_scr", [NS, 128, HID], F32, kind=scr_kind).ap()
    oq_scr = nc.dram_tensor("oq_scr", [NJC, 128, DM], BF16, kind=scr_kind).ap()

    ctx = contextlib.ExitStack()
    A = ctx.enter_context(nc.sbuf_tensor("arena_f32", [128, A_W], F32))
    B = ctx.enter_context(nc.sbuf_tensor("arena_bf16", [128, B_W], BF16))

    def af(off, w, p0=0, p1=128):
        return A[p0:p1, off:off + w]

    def ab(off, w, p0=0, p1=128):
        return B[p0:p1, off:off + w]

    idf = ctx.enter_context(nc.sbuf_tensor("idf_s", [128, 128], F32))
    idb = ctx.enter_context(nc.sbuf_tensor("idb_s", [128, 128], BF16))
    ones1 = ctx.enter_context(nc.sbuf_tensor("ones1", [1, 128], F32))
    onesc = ctx.enter_context(nc.sbuf_tensor("onesc", [128, 1], F32))
    gxs = ctx.enter_context(nc.sbuf_tensor("gxs", [128, NS], F32))
    g11 = ctx.enter_context(nc.sbuf_tensor("g11", [1, 8], F32))

    p_a = ctx.enter_context(nc.psum_tensor("p_a", [128, 2048], F32))
    p_b = ctx.enter_context(nc.psum_tensor("p_b", [128, 1024], F32))
    p_tr = ctx.enter_context(nc.psum_tensor("p_tr", [128, 2048], BF16))

    P = Prog()
    semnames = ["PE", "DV", "AC", "GP", "L0",
                "LU0", "LU1", "LT0", "LT1", "LA0", "LA1", "LW0", "LW1",
                "LQ0", "LQ1", "LH0", "LH1", "LB0", "LB1", "LX0", "LX1",
                "LR0", "LR1",
                "SX0", "SX1", "SH0", "SH1", "SQ0", "SQ1", "SO0", "SO1"]
    for nm in semnames:
        P.sems[nm] = ctx.enter_context(nc.semaphore("s" + nm))

    cols = af(O_COLS, NCOL)

    def col(i):
        return A[:, O_COLS + i:O_COLS + i + 1]

    def ccol(base, i):
        return A[:, O_COLS + base + i:O_COLS + base + i + 1]

    M_ = O_WL
    wbrT = af(M_, 1024)
    wcm_f = af(M_ + 1024, DM, 0, 64)
    G_f = af(M_ + 3072, 64, 0, 64)
    t1b = af(M_ + 3136, 128, 0, 64)
    t2b = af(M_ + 3264, 128, 0, 64)
    wbs_t = af(O_PQ, 256, 0, 64)
    bb0 = af(O_PQ + 256, 256, 0, 64)
    bb1 = af(O_PQ + 512, 256, 0, 64)
    tmp256 = af(O_PQ + 768, 256, 0, 64)
    a_bc = af(O_PQ + 1024, 32, 0, 64)
    a_row = af(O_PQ + 1056, 32, 0, 1)
    btT = af(O_PQ + 1088, 128)
    wcr0 = af(O_PQ + 1216, 64)
    wcr1 = af(O_PQ + 1280, 64)
    d_row = af(O_RES, DM, 0, 1)
    dp1 = af(O_DP1, DM)

    ld0 = [(idf[:, :], idf_h[:, :]), (idb[:, :], idb_h[:, :]),
           (d_row, d_h[:, :]), (wbs_t, wbs_h[:, :]), (a_row, a_h[:, :]),
           (wcm_f, wcm_h[:, :]),
           (wbrT.rearrange("p (k r) -> p k r", r=64),
            wbr_h.rearrange("(k p) r -> p k r", p=128)),
           (wcr0, wcr_h[0:128, :]), (wcr1, wcr_h[128:256, :])]
    for dst, src in ld0:
        P.inc("sync", "L0", lambda e, d=dst, s=src: e.dma_start(d, s), 16)
    ld0_done = P.val("L0")

    P.do("vector", lambda e: e.memset(col(C_EPS), float(EPS)))
    P.do("vector", lambda e: e.memset(col(C_HALF), 0.5))
    for j, kk in enumerate(LADK):
        P.do("vector", lambda e, j=j, kk=kk: e.memset(col(C_K0 + j), -kk / 2.0))
    P.do("vector", lambda e: e.memset(ones1[:1, :], 1.0))
    P.do("vector", lambda e: e.memset(onesc[:, :], 1.0))
    P.do("vector", lambda e: e.memset(cols[:, C_RA0:C_RA0 + 3], 0.0))
    P.w("vector", "L0", ld0_done)
    dv_c0 = P.inc("vector", "DV", lambda e: (e.drain(), e.engine_nop())[1])

    P.w("tensor", "DV", dv_c0)
    pe_abc = P.inc("tensor", "PE", lambda e: e.matmul(
        p_b[0:128, 0:32], ones1[:1, 0:128], a_row, start=True, stop=True))
    P.w("vector", "PE", pe_abc)
    P.do("vector", lambda e: e.tensor_copy(a_bc, p_b[0:64, 0:32]))
    P.do("vector", lambda e: e.drain())
    P.do("vector", lambda e: e.tensor_copy(bb0, wbs_t))
    cur, nxt = bb0, bb1
    for lvl in range(8):
        wwid = 2 ** (7 - lvl)
        vi = cur.rearrange("p (b t w) -> p b t w", t=2, w=wwid)
        vo = nxt.rearrange("p (b t w) -> p b t w", t=2, w=wwid)
        vt = tmp256.rearrange("p (b t w) -> p b t w", t=2, w=wwid)
        for j in range(2):
            a0 = a_bc[:, lvl * 4 + j: lvl * 4 + j + 1]
            a1 = a_bc[:, lvl * 4 + 2 + j: lvl * 4 + 3 + j]
            P.do("vector", lambda e: e.drain())
            P.do("vector", lambda e, o=vt[:, :, j, :], i=vi[:, :, 0, :], s=a0:
                 e.tensor_single_scalar(o, i, s, ALU.mult))
            P.do("vector", lambda e: e.drain())
            P.do("vector", lambda e, o=vo[:, :, j, :], i=vi[:, :, 1, :], s=a1,
                 t=vt[:, :, j, :]:
                 e.scalar_tensor_tensor(o, i, s, t, ALU.mult, ALU.add))
        cur, nxt = nxt, cur
    dv_bfly = P.inc("vector", "DV", lambda e: (e.drain(), e.engine_nop())[1])

    P.w("tensor", "DV", dv_bfly)
    P.do("tensor", lambda e, i=cur: e.transpose(p_b[:, 0:64], i[:, 0:128],
                                                idf[0:64, 0:64]))
    pe_btt = P.inc("tensor", "PE", lambda e, i=cur: e.transpose(
        p_b[:, 64:128], i[:, 128:256], idf[0:64, 0:64]))
    P.w("vector", "PE", pe_btt)
    dv_btt = P.inc("vector", "DV", lambda e: e.tensor_copy(btT, p_b[:, 0:128]))
    P.w("tensor", "DV", dv_btt)
    P.do("tensor", lambda e: e.matmul(p_b[0:64, 256:320], btT[0:128, 0:64],
                                      wcr0, start=True, stop=False))
    pe_g = P.inc("tensor", "PE", lambda e: e.matmul(
        p_b[0:64, 256:320], btT[0:128, 64:128], wcr1, start=False, stop=True))
    P.w("vector", "PE", pe_g)
    dv_g = P.inc("vector", "DV", lambda e: e.tensor_copy(G_f, p_b[0:64, 256:320]))

    P.w("tensor", "DV", dv_g)
    dv_dpe = {}
    for i in range(4):
        if i >= 2:
            P.w("tensor", "DV", dv_dpe[i - 2])
        pe_d = P.inc("tensor", "PE", lambda e, i=i: e.matmul(
            p_a[:, (i % 2) * 512:(i % 2) * 512 + 512], ones1[:1, :],
            d_row[:, i * 512:(i + 1) * 512], start=True, stop=True))
        P.w("vector", "PE", pe_d)
        dv_dpe[i] = P.inc("vector", "DV", lambda e, i=i: e.tensor_scalar(
            dp1[:, i * 512:(i + 1) * 512],
            p_a[:, (i % 2) * 512:(i % 2) * 512 + 512],
            1.0, 1.0, ALU.mult, ALU.add))
    dv_dp1 = P.inc("vector", "DV", lambda e: (e.drain(), e.engine_nop())[1])

    ubuf = af(O_UB, DM)
    utb = af(O_UT, DM)
    res = af(O_RES, DM)
    x1b = af(O_X1, DM)
    lC = af(O_LC, DM)
    lY = af(O_LY, DM)
    sgt = ab(B_SG, DM)
    xt_bf = ab(B_OQ, DM)
    xi_all = B[:, B_XI:B_XI + NS * DM].rearrange("p (k t) -> p k t", t=TOK)
    pA_w = [af(O_GS, DM), af(O_GS + 2048, DM)]

    pa_chunks = []
    for mi, wh in ((0, gwt_h), (1, vwt_h)):
        fl = wh.rearrange("(c p) x -> c p x", p=128)
        for c in range(16):
            for s4 in range(4):
                pa_chunks.append((fl[c, :, s4 * 2048:(s4 + 1) * 2048], mi))
    flo = owt_h.rearrange("(c p) x -> c p x", p=128)
    for c in range(64):
        pa_chunks.append((flo[c, :, :], 2))
    NPA = len(pa_chunks)
    per = (NPA + NS - 1) // NS
    pa_sched = [list(range(s * per, min((s + 1) * per, NPA))) for s in range(NS)]

    la_ac = {}
    gp_add = {}
    abs_scr = [ab(B_GWA, DM), ab(B_GWA + 2048, DM)]

    def emit_passA(idxs):
        for j in idxs:
            src, mi = pa_chunks[j]
            bp = j % 2
            if j >= 2:
                P.w("sync", "AC", la_ac[j - 2])
            la = P.inc("sync", "LA%d" % bp,
                       lambda e, d=pA_w[bp], s=src: e.dma_start(d, s), 16)
            P.w("scalar", "LA%d" % bp, la)
            if j >= 2:
                P.w("scalar", "GP", gp_add[j - 2])
            acc = ccol(0, (C_SA0 if j % 2 == 0 else C_SB0) + mi)
            la_ac[j] = P.inc("scalar", "AC", lambda e, bp=bp, acc=acc, j=j:
                             e.activation(abs_scr[j % 2], pA_w[bp], AF.Abs,
                                          accum_out=acc))
            P.w("gpsimd", "AC", la_ac[j])
            gp_add[j] = P.inc("gpsimd", "GP", lambda e, mi=mi, acc=acc:
                              e.tensor_tensor(ccol(0, C_RA0 + mi),
                                              ccol(0, C_RA0 + mi), acc,
                                              ALU.add))
            P.do("gpsimd", lambda e: e.drain())

    st_x1 = {}
    dv_xi_done = {}
    dv_ures_s = {}
    pe_t1_s = {}
    dv_resadd = {}
    for s in range(NS):
        bp = s % 2
        if s >= 1:
            P.w("sync", "DV", dv_ures_s[s - 1])
        lu = P.inc("sync", "LU%d" % bp, lambda e, s=s: e.dma_start(
            ubuf, u_h[s * 128:(s + 1) * 128, :]), 16)
        if s >= 1:
            P.w("sync", "PE", pe_t1_s[s - 1])
        lt = P.inc("sync", "LT%d" % bp, lambda e, s=s: e.dma_start(
            utb.rearrange("p (k t) -> p k t", t=128),
            ut_h.rearrange("(k p) t -> p k t", p=128)
            [:, :, s * 128:(s + 1) * 128]), 16)
        P.w("tensor", "LT%d" % bp, lt)
        if s == 0:
            P.w("tensor", "DV", dv_dp1)
        for k in range(NK):
            fn = lambda e, k=k: e.matmul(
                p_b[0:64, 0:128], wbrT[:, k * 64:(k + 1) * 64],
                utb[:, k * 128:(k + 1) * 128], start=(k == 0), stop=(k == 15))
            if k == 15:
                pe_t1 = P.inc("tensor", "PE", fn)
            else:
                P.do("tensor", fn)
        pe_t1_s[s] = pe_t1
        P.w("vector", "PE", pe_t1)
        dv_t1 = P.inc("vector", "DV", lambda e: e.tensor_copy(
            t1b, p_b[0:64, 0:128]))
        P.w("tensor", "DV", dv_t1)
        pe_t2 = P.inc("tensor", "PE", lambda e: e.matmul(
            p_b[0:64, 512:640], G_f, t1b[0:64, :], start=True, stop=True))
        P.w("vector", "PE", pe_t2)
        dv_t2 = P.inc("vector", "DV", lambda e: e.tensor_copy(
            t2b, p_b[0:64, 512:640]))
        P.w("tensor", "DV", dv_t2)
        pe_y = {}
        for g in range(4):
            if s >= 1:
                P.w("tensor", "DV", dv_resadd[(s - 1, g)])
            pe_y[g] = P.inc("tensor", "PE", lambda e, g=g: e.matmul(
                p_a[:, g * 512:(g + 1) * 512], t2b[0:64, :],
                wcm_f[:, g * 512:(g + 1) * 512], start=True, stop=True))
        P.w("vector", "LU%d" % bp, lu)
        P.do("vector", lambda e: e.drain())
        P.do("vector", lambda e: e.tensor_tensor(res, ubuf, dp1, ALU.mult))
        dv_ures_s[s] = P.inc("vector", "DV",
                             lambda e: (e.drain(), e.engine_nop())[1])
        for g in range(4):
            P.w("vector", "PE", pe_y[g])
            P.do("vector", lambda e: e.drain())
            dv_resadd[(s, g)] = P.inc("vector", "DV", lambda e, g=g:
                                      e.tensor_tensor(
                                          res[:, g * 512:(g + 1) * 512],
                                          res[:, g * 512:(g + 1) * 512],
                                          p_a[:, g * 512:(g + 1) * 512],
                                          ALU.add))
        mcol, vcol, rcol = col(C_T0), col(C_T1), col(C_T3)
        P.do("vector", lambda e: e.drain())
        P.do("vector", lambda e: e.tensor_reduce(mcol, res, AX.X, ALU.add))
        P.do("vector", lambda e: e.drain())
        P.do("vector", lambda e: e.tensor_single_scalar(mcol, mcol, INV_D,
                                                        ALU.mult))
        P.do("vector", lambda e: e.drain())
        dv_ln = P.inc("vector", "DV", lambda e: e.tensor_single_scalar(
            res, res, mcol, ALU.subtract))
        P.w("scalar", "DV", dv_ln)
        P.do("scalar", lambda e: e.activation(sgt, res, AF.Square,
                                              accum_out=vcol))
        P.do("scalar", lambda e: e.drain())
        P.do("scalar", lambda e: e.activation(col(C_T2), vcol, AF.Ln,
                                              bias=col(C_EPS), scale=INV_D))
        P.do("scalar", lambda e: e.drain())
        ac_ln = P.inc("scalar", "AC", lambda e: e.activation(
            rcol, col(C_T2), AF.Exp, scale=-0.5))
        P.w("vector", "AC", ac_ln)
        if s >= 1:
            P.w("vector", "SX%d" % ((s - 1) % 2), st_x1[s - 1])
        P.do("vector", lambda e: e.drain())
        dv_x1 = P.inc("vector", "DV", lambda e: e.tensor_single_scalar(
            x1b, res, rcol, ALU.mult))
        P.w("sync", "DV", dv_x1)
        st_x1[s] = P.inc("sync", "SX%d" % bp, lambda e, s=s: e.dma_start(
            x1_scr[s, :, :], x1b), 16)
        gxc = gxs[:, s:s + 1]
        sxc = ccol(C_SXS, s)
        P.do("vector", lambda e: e.drain())
        P.do("vector", lambda e, g=gxc: e.tensor_reduce(
            g, x1b, AX.X, ALU.max, apply_absolute_value=True))
        P.do("vector", lambda e: e.drain())
        P.do("vector", lambda e, g=gxc: e.tensor_single_scalar(
            g, g, 1e-5, ALU.max))
        P.do("vector", lambda e: e.drain())
        P.do("vector", lambda e, g=gxc, s=s: e.tensor_single_scalar(
            ccol(C_GXI, s), g, 1.0 / 127.0, ALU.mult))
        P.do("vector", lambda e: e.drain())
        P.do("vector", lambda e, sc=sxc, s=s: e.reciprocal(
            sc, ccol(C_GXI, s)))
        P.do("vector", lambda e: e.drain())
        dv_prev = P.inc("vector", "DV", lambda e, sc=sxc:
                        e.tensor_single_scalar(lY, x1b, sc, ALU.mult))
        for j in range(8):
            src_c = lY if j == 0 else lC
            P.w("scalar", "DV", dv_prev)
            ac_j = P.inc("scalar", "AC", lambda e, src_c=src_c: e.activation(
                sgt, src_c, AF.Sign, bias=col(C_HALF)))
            P.w("vector", "AC", ac_j)
            dv_prev = P.inc("vector", "DV", lambda e, j=j, src_c=src_c:
                            e.scalar_tensor_tensor(lC, sgt, col(C_K0 + j),
                                                   src_c, ALU.mult, ALU.add))
        P.do("vector", lambda e: e.drain())
        P.do("vector", lambda e: e.tensor_tensor(res, lY, lC, ALU.subtract))
        P.do("vector", lambda e: e.drain())
        if s >= 1:
            P.w("vector", "PE", pe_xt_s)
        dv_xt = P.inc("vector", "DV", lambda e: e.tensor_scalar(
            xt_bf, res, 1.0, -0.5, ALU.mult, ALU.add))
        P.w("tensor", "DV", dv_xt)
        if s >= 1:
            P.w("tensor", "DV", dv_xi_done[s - 1])
        for k in range(16):
            fn = lambda e, k=k: e.transpose(
                p_tr[:, k * 128:(k + 1) * 128], xt_bf[:, k * 128:(k + 1) * 128],
                idb[:, :])
            if k == 15:
                pe_xt_s = P.inc("tensor", "PE", fn)
            else:
                P.do("tensor", fn)
        P.w("vector", "PE", pe_xt_s)
        ptr_view = p_tr[:, :].rearrange("p (k t) -> p k t", t=128)
        dv_xi_done[s] = P.inc(
            "vector", "DV",
            lambda e, s=s, o=xi_all[:, :, s * 128:(s + 1) * 128], i=ptr_view:
            e.tensor_copy(o, i))
        emit_passA(pa_sched[s])

    pe_after_p1 = P.val("PE")

    P.w("tensor", "GP", P.val("GP"))
    pe_g3 = {}
    for mi in range(3):
        pe_g3[mi] = P.inc("tensor", "PE", lambda e, mi=mi: e.matmul(
            p_b[0:1, 640 + mi:641 + mi], ccol(0, C_RA0 + mi), onesc[:, :],
            start=True, stop=True))
    P.w("vector", "PE", pe_g3[2])
    P.do("vector", lambda e: e.tensor_scalar(
        g11[:1, 0:3], p_b[0:1, 640:643], 1.0 / 16777216.0, 1e-5,
        ALU.mult, ALU.max))
    dv_g11 = P.inc("vector", "DV", lambda e: (e.drain(), e.engine_nop())[1])
    P.w("tensor", "DV", dv_g11)
    pe_gb = P.inc("tensor", "PE", lambda e: e.matmul(
        p_b[:, 644:647], ones1[:1, :], g11[:1, 0:3], start=True, stop=True))
    P.w("vector", "PE", pe_gb)
    for gi, (cth, csc, cis) in enumerate(
            [(C_SCG, C_SCG, C_ISG), (C_SCV, C_SCV, C_ISV), (C_SCO, C_SCO, -1)]):
        P.do("vector", lambda e, gi=gi, csc=csc: e.tensor_single_scalar(
            col(csc), p_b[:, 644 + gi:645 + gi], 0.5, ALU.mult))
        P.do("vector", lambda e: e.drain())
        if cis >= 0:
            P.do("vector", lambda e, csc=csc, cis=cis: e.reciprocal(
                col(cis), col(csc)))
            P.do("vector", lambda e: e.drain())
    TH = {}
    for mi, csc in ((0, C_SCG), (1, C_SCV), (2, C_SCO)):
        cp = C_SA0 + mi
        cn = C_SB0 + mi
        P.do("vector", lambda e, cp=cp, csc=csc: e.tensor_copy(
            col(cp), col(csc)))
        P.do("vector", lambda e: e.drain())
        P.do("vector", lambda e, cp=cp, cn=cn: e.tensor_single_scalar(
            col(cn), col(cp), -1.0, ALU.mult))
        P.do("vector", lambda e: e.drain())
        TH[mi] = (cp, cn)
    P.do("vector", lambda e: e.tensor_single_scalar(
        cols[:, C_SGC:C_SGC + 16], cols[:, C_GXI:C_GXI + 16], col(C_SCG),
        ALU.mult))
    P.do("vector", lambda e: e.drain())
    P.do("vector", lambda e: e.tensor_single_scalar(
        cols[:, C_SVC:C_SVC + 16], cols[:, C_GXI:C_GXI + 16], col(C_SCV),
        ALU.mult))
    P.do("vector", lambda e: e.drain())
    rows_g = lambda s: af(O_RES + s * 128, 128, 0, 1)
    rows_v = lambda s: af(O_X1 + s * 128, 128, 0, 1)
    P.w("vector", "SX0", st_x1[NS - 2])
    P.w("vector", "SX1", st_x1[NS - 1])
    for mi in range(2):
        base = C_SGC if mi == 0 else C_SVC
        rows = rows_g if mi == 0 else rows_v
        for s in range(NS):
            P.do("vector", lambda e, base=base, s=s: e.reciprocal(
                col(C_T0), ccol(base, s)))
            P.do("vector", lambda e: e.drain())
            dv_iv = P.inc("vector", "DV", lambda e: (e.drain(),
                                                     e.engine_nop())[1])
            P.w("tensor", "DV", dv_iv)
            pe_tv = P.inc("tensor", "PE", lambda e: e.transpose(
                p_b[0:1, 648:776], col(C_T0), idf[:, :]))
            P.w("vector", "PE", pe_tv)
            P.do("vector", lambda e: e.drain())
            P.inc("vector", "DV", lambda e, ro=rows(s): e.tensor_copy(
                ro, p_b[0:1, 648:776]))
    dv_rows = P.inc("vector", "DV", lambda e: (e.drain(), e.engine_nop())[1])

    wlb = [af(O_WL, DM), af(O_WL + 2048, DM)]
    pqb = af(O_PQ, DM)
    gs_slab = af(O_GS, 8192)
    sgh = [ab(B_SG, 1024), ab(B_SG + 1024, 1024)]
    oqb = ab(B_OQ, DM)
    gwq = B[:, B_GWA:B_GWA + 8192]
    hbuf2 = [af(O_UB, 512), af(O_UB + 512, 512)]
    brb = [af(O_BR, 512, 0, 1), af(O_BR + 512, 512, 0, 1)]
    P.do("vector", lambda e: e.memset(cols[:, C_MAC:C_MAC + 16], 0.0))
    dv_mz = P.inc("vector", "DV", lambda e: (e.drain(), e.engine_nop())[1])

    pq_ac = {}
    pq_st = {}
    chain = {"dv": 0}

    def emit_prequant(jcs):
        for jc in jcs:
            if jc >= NJC:
                continue
            if jc >= 1:
                P.w("sync", "AC", pq_ac[jc - 1])
            lq = P.inc("sync", "LQ%d" % (jc % 2), lambda e, jc=jc:
                       e.dma_start(pqb, flo[jc, :, :]), 16)
            P.w("scalar", "LQ%d" % (jc % 2), lq)
            for hh in range(2):
                sl = slice(hh * 1024, (hh + 1) * 1024)
                if chain["dv"]:
                    P.w("scalar", "DV", chain["dv"])
                P.do("scalar", lambda e, sl=sl: e.activation(
                    sgh[0], pqb[:, sl], AF.Sign, bias=col(C_SB0 + 2)))
                acq = P.inc("scalar", "AC", lambda e, sl=sl: e.activation(
                    sgh[1], pqb[:, sl], AF.Sign, bias=col(C_SA0 + 2)))
                P.w("vector", "AC", acq)
                if jc >= 1 and hh == 0:
                    P.w("vector", "SQ%d" % ((jc - 1) % 2), pq_st[jc - 1])
                dvq = P.inc("vector", "DV", lambda e, sl=sl: e.tensor_tensor(
                    oqb[:, sl], sgh[0], sgh[1], ALU.add))
                chain["dv"] = dvq
            pq_ac[jc] = acq
            P.w("sync", "DV", dvq)
            pq_st[jc] = P.inc("sync", "SQ%d" % (jc % 2),
                              lambda e, jc=jc: e.dma_start(oq_scr[jc, :, :],
                                                           oqb), 16)

    P.w("scalar", "DV", dv_rows)
    P.w("sync", "DV", dv_rows)
    P.w("sync", "PE", pe_after_p1)
    gwt_r = gwt_h.rearrange("(k p) x -> p k x", p=128)
    vwt_r = vwt_h.rearrange("(k p) x -> p k x", p=128)
    st_h = {}
    lw_ac = {}
    lw_dv = {}
    pe_step_done = {}
    dv_epi = {}
    ac_epi = {}
    lr_hist = {}
    pq_next = 0
    step = 0
    dvs = 0
    for hg in range(NHG):
        for mat in range(2):
            wr = gwt_r if mat == 0 else vwt_r
            cp, cn = TH[mat]
            gr = gbr_h if mat == 0 else vbr_h
            if step >= 2:
                P.w("sync", "PE", pe_step_done[step - 2])
            lr = P.inc("sync", "LR%d" % (step % 2), lambda e, gr=gr, hg=hg,
                       b=brb[step % 2]: e.dma_start(
                           b, gr[:, hg * 512:(hg + 1) * 512]), 16)
            lr_hist[step] = lr
            for c in range(4):
                j = step * 4 + c
                bp = j % 2
                if j >= 2:
                    P.w("sync", "AC", lw_ac[j - 2])
                lw = P.inc("sync", "LW%d" % bp, lambda e, wr=wr, c=c, hg=hg,
                           bp=bp: e.dma_start(
                               wlb[bp].rearrange("p (k x) -> p k x", x=512),
                               wr[:, c * 4:c * 4 + 4,
                                  hg * 512:(hg + 1) * 512]), 16)
                P.w("scalar", "LW%d" % bp, lw)
                if step >= 1 and c == 0:
                    P.w("vector", "PE", pe_step_done[step - 1])
                for hh in range(2):
                    sl = slice(hh * 1024, (hh + 1) * 1024)
                    if chain["dv"]:
                        P.w("scalar", "DV", chain["dv"])
                    P.do("scalar", lambda e, bp=bp, sl=sl, cn=cn:
                         e.activation(sgh[0], wlb[bp][:, sl], AF.Sign,
                                      bias=col(cn)))
                    acs = P.inc("scalar", "AC", lambda e, bp=bp, sl=sl, cp=cp:
                                e.activation(sgh[1], wlb[bp][:, sl], AF.Sign,
                                             bias=col(cp)))
                    P.w("vector", "AC", acs)
                    dvs2 = P.inc("vector", "DV", lambda e, c=c, sl=sl:
                                 e.tensor_tensor(
                                     gwq[:, c * 2048 + sl.start:
                                         c * 2048 + sl.stop],
                                     sgh[0], sgh[1], ALU.add))
                    chain["dv"] = dvs2
                lw_ac[j] = acs
                lw_dv[j] = dvs2
            dv_q_all = P.val("DV")
            P.w("tensor", "DV", dv_q_all)
            if step == 0:
                P.w("tensor", "DV", dv_xi_done[NS - 1])
                P.w("tensor", "DV", dv_mz)
            P.w("tensor", "LR%d" % (step % 2), lr_hist[step])
            for t in range(16):
                bank = t % 4
                if t >= 4:
                    P.w("tensor", "AC" if mat == 0 else "DV",
                        (ac_epi if mat == 0 else dv_epi)[(step, t - 4)])
                elif step >= 1:
                    prev = (step - 1, t + 12)
                    P.w("tensor", "AC" if (step - 1) % 2 == 0 else "DV",
                        (ac_epi if (step - 1) % 2 == 0 else dv_epi)[prev])
                ro = (rows_g if mat == 0 else rows_v)(t)
                P.do("tensor", lambda e, bank=bank, ro=ro, b=brb[step % 2]:
                     e.matmul(p_a[:, bank * 512:(bank + 1) * 512], ro, b,
                              start=True, stop=False))
                for k in range(NK):
                    fn = lambda e, k=k, t=t, bank=bank: e.matmul(
                        p_a[:, bank * 512:(bank + 1) * 512],
                        B[:, B_XI + k * 2048 + t * 128:
                           B_XI + k * 2048 + t * 128 + 128],
                        gwq[:, k * 512:(k + 1) * 512],
                        start=False, stop=(k == 15))
                    if k == 15:
                        pe_mm = P.inc("tensor", "PE", fn)
                    else:
                        P.do("tensor", fn)
                if mat == 0:
                    P.w("scalar", "PE", pe_mm)
                    if step >= 2:
                        P.w("scalar", "DV", dv_epi[(step - 1, t)])
                    ac_epi[(step, t)] = P.inc(
                        "scalar", "AC", lambda e, t=t, bank=bank:
                        e.activation(gs_slab[:, t * 512:(t + 1) * 512],
                                     p_a[:, bank * 512:(bank + 1) * 512],
                                     AF.Silu, scale=ccol(C_SGC, t)))
                else:
                    P.w("vector", "PE", pe_mm)
                    hp = t % 2
                    if t >= 2:
                        P.w("vector", "SH%d" % hp, st_h[(hg, t - 2)])
                    elif hg >= 1:
                        P.w("vector", "SH%d" % hp, st_h[(hg - 1, t + 14)])
                    if t == 0:
                        P.w("vector", "AC", ac_epi[(step - 1, 15)])
                    P.do("vector", lambda e: e.drain())
                    P.do("vector", lambda e, t=t, bank=bank, hp=hp:
                         e.scalar_tensor_tensor(
                             hbuf2[hp], p_a[:, bank * 512:(bank + 1) * 512],
                             ccol(C_SVC, t), gs_slab[:, t * 512:(t + 1) * 512],
                             ALU.mult, ALU.mult))
                    P.do("vector", lambda e: e.drain())
                    P.do("vector", lambda e, hp=hp: e.tensor_reduce(
                        col(C_T0), hbuf2[hp], AX.X, ALU.max,
                        apply_absolute_value=True))
                    P.do("vector", lambda e: e.drain())
                    dv_h = P.inc("vector", "DV", lambda e, t=t:
                                 e.tensor_tensor(ccol(C_MAC, t),
                                                 ccol(C_MAC, t), col(C_T0),
                                                 ALU.max))
                    dv_epi[(step, t)] = dv_h
                    P.w("sync", "DV", dv_h)
                    st_h[(hg, t)] = P.inc(
                        "sync", "SH%d" % hp,
                        lambda e, t=t, hg=hg, hp=hp: e.dma_start(
                            h_scr[t, :, hg * 512:(hg + 1) * 512], hbuf2[hp]),
                        16)
            pe_step_done[step] = P.val("PE")
            step += 1
        emit_prequant(list(range(pq_next, pq_next + 4)))
        pq_next += 4

    P.w("vector", "AC", pq_ac[NJC - 1])
    P.do("vector", lambda e: e.drain())
    P.do("vector", lambda e: e.tensor_scalar(
        cols[:, C_MAC:C_MAC + 16], cols[:, C_MAC:C_MAC + 16], 1.0, 1e-5,
        ALU.mult, ALU.max))
    P.do("vector", lambda e: e.drain())
    P.do("vector", lambda e: e.tensor_scalar(
        cols[:, C_S2:C_S2 + 16], cols[:, C_MAC:C_MAC + 16], 1.0 / 127.0, 0.0,
        ALU.mult, ALU.add))
    P.do("vector", lambda e: e.drain())
    P.do("vector", lambda e: e.reciprocal(
        cols[:, C_MAC:C_MAC + 16], cols[:, C_S2:C_S2 + 16]))
    P.do("vector", lambda e: e.drain())
    P.do("vector", lambda e: e.tensor_single_scalar(
        cols[:, C_S2:C_S2 + 16], cols[:, C_S2:C_S2 + 16], col(C_SCO),
        ALU.mult))
    P.do("vector", lambda e: e.drain())
    rows_o = lambda s: af(O_PQ + s * 128, 128, 0, 1)
    for s in range(NS):
        P.do("vector", lambda e, s=s: e.reciprocal(col(C_T0), ccol(C_S2, s)))
        P.do("vector", lambda e: e.drain())
        dv_iv = P.inc("vector", "DV", lambda e: (e.drain(),
                                                 e.engine_nop())[1])
        P.w("tensor", "DV", dv_iv)
        if s == 0:
            P.w("tensor", "PE", pe_step_done[31])
        pe_tv = P.inc("tensor", "PE", lambda e: e.transpose(
            p_b[0:1, 648:776], col(C_T0), idf[:, :]))
        P.w("vector", "PE", pe_tv)
        P.do("vector", lambda e: e.drain())
        P.inc("vector", "DV", lambda e, ro=rows_o(s): e.tensor_copy(
            ro, p_b[0:1, 648:776]))
    dv_orows = P.inc("vector", "DV", lambda e: (e.drain(), e.engine_nop())[1])

    hb = [af(O_UB, DM), af(O_UT, DM)]
    x1r = [af(O_GS, DM), af(O_GS + 2048, DM), af(O_GS + 4096, DM),
           af(O_GS + 6144, DM)]
    zpart = [af(O_DP1, DM), af(O_X1, DM), af(O_WL, DM), af(O_WL + 2048, DM)]
    xtmp = af(O_RES, DM)
    xip = ab(B_OQ, DM)
    hit = B[:, B_XI:B_XI + NJC * 512]
    hit_v = hit.rearrange("p (jc t) -> p jc t", t=512)
    owqb = [B[:, B_GWA + i * 4096:B_GWA + (i + 1) * 4096] for i in range(2)]
    oq_r = oq_scr.rearrange("j p d -> p j d")

    lh_ct = 0
    dv_lad = {}
    pe_tr_d = {}
    dv_evac = {}
    st_out = {}
    dv_zmm = {}
    dv_zdone = {}
    lx_hist = {}
    lb_hist = {}
    pe_q_mm = {}
    first_dv_evac_of_q = {}
    for q in range(NQ):
        for c in range(4):
            for ss in range(4):
                s = q * 4 + ss
                bp = lh_ct % 2
                if lh_ct >= 2:
                    P.w("sync", "DV", dv_lad[lh_ct - 2])
                if q == 0 and c == 0 and ss == 0:
                    P.w("sync", "SH0", P.val("SH0"))
                    P.w("sync", "SH1", P.val("SH1"))
                lh = P.inc("sync", "LH%d" % bp, lambda e, s=s, c=c, bp=bp:
                           e.dma_start(hb[bp],
                                       h_scr[s, :, c * 2048:(c + 1) * 2048]),
                           16)
                P.w("vector", "LH%d" % bp, lh)
                if q == 0 and c == 0 and ss == 0:
                    P.w("vector", "DV", dv_orows)
                    P.w("vector", "SQ0", P.val("SQ0"))
                    P.w("vector", "SQ1", P.val("SQ1"))
                r2c = ccol(C_MAC, s)
                P.do("vector", lambda e: e.drain())
                dv_prev = P.inc("vector", "DV", lambda e, bp=bp, r2c=r2c:
                                e.tensor_single_scalar(lY, hb[bp], r2c,
                                                       ALU.mult))
                for j in range(8):
                    src_c = lY if j == 0 else lC
                    P.w("scalar", "DV", dv_prev)
                    ac_j = P.inc("scalar", "AC", lambda e, src_c=src_c:
                                 e.activation(sgt, src_c, AF.Sign,
                                              bias=col(C_HALF)))
                    P.w("vector", "AC", ac_j)
                    dv_prev = P.inc("vector", "DV",
                                    lambda e, j=j, src_c=src_c:
                                    e.scalar_tensor_tensor(
                                        lC, sgt, col(C_K0 + j), src_c,
                                        ALU.mult, ALU.add))
                P.do("vector", lambda e: e.drain())
                P.do("vector", lambda e: e.tensor_tensor(xtmp, lY, lC,
                                                         ALU.subtract))
                P.do("vector", lambda e: e.drain())
                if lh_ct >= 1:
                    P.w("vector", "PE", pe_tr_d[lh_ct - 1])
                dv_lad[lh_ct] = P.inc("vector", "DV", lambda e:
                                      e.tensor_scalar(xip, xtmp, 1.0, -0.5,
                                                      ALU.mult, ALU.add))
                P.w("tensor", "DV", dv_lad[lh_ct])
                if lh_ct >= 1:
                    P.w("tensor", "DV", dv_evac[lh_ct - 1])
                for kk in range(16):
                    fn = lambda e, kk=kk: e.transpose(
                        p_tr[:, kk * 128:(kk + 1) * 128],
                        xip[:, kk * 128:(kk + 1) * 128], idb[:, :])
                    if kk == 15:
                        pe_tr = P.inc("tensor", "PE", fn)
                    else:
                        P.do("tensor", fn)
                pe_tr_d[lh_ct] = pe_tr
                P.w("vector", "PE", pe_tr)
                if q > 0 and c == 0 and ss == 0:
                    P.w("vector", "PE", pe_q_mm[q - 1])
                ptr_view = p_tr[:, :].rearrange("p (k t) -> p k t", t=128)
                dv_evac[lh_ct] = P.inc(
                    "vector", "DV",
                    lambda e, c=c, ss=ss, i=ptr_view:
                    e.tensor_copy(hit_v[:, c * 16:(c + 1) * 16,
                                        ss * 128:(ss + 1) * 128], i))
                lh_ct += 1
        dv_lad_all = P.val("DV")
        P.w("tensor", "DV", dv_lad_all)
        for dmg in range(4):
            nd = q * 4 + dmg
            if nd >= 2:
                P.w("sync", "PE", lb_hist[((nd - 2) // 4, (nd - 2) % 4, 0)])
            else:
                P.w("sync", "PE", pe_step_done[2 * NHG - 1])
            lrq = P.inc("sync", "LR%d" % (nd % 2), lambda e, dmg=dmg,
                        b=brb[nd % 2]: e.dma_start(
                            b, obr_h[:, dmg * 512:(dmg + 1) * 512]), 16)
            for grp in range(8):
                gbp = grp % 2
                key = (q, dmg, grp)
                nload = nd * 8 + grp
                if nload >= 2:
                    pk = nload - 2
                    P.w("sync", "PE", lb_hist[(pk // 32, (pk // 8) % 4,
                                               pk % 8)])
                if q == 0 and dmg == 0 and grp == 0:
                    P.w("sync", "SQ0", P.val("SQ0"))
                    P.w("sync", "SQ1", P.val("SQ1"))
                lb = P.inc("sync", "LB%d" % gbp, lambda e, dmg=dmg, grp=grp,
                           gbp=gbp: e.dma_start(
                               owqb[gbp].rearrange("p (j d) -> p j d", d=512),
                               oq_r[:, grp * 8:grp * 8 + 8,
                                    dmg * 512:(dmg + 1) * 512]), 16)
                P.w("tensor", "LB%d" % gbp, lb)
                if grp == 0:
                    P.w("tensor", "LR%d" % (nd % 2), lrq)
                    if dmg >= 1:
                        P.w("tensor", "DV", dv_zmm[(q, dmg - 1, 3)])
                    elif q > 0:
                        P.w("tensor", "DV", dv_zmm[(q - 1, 3, 3)])
                    for t in range(4):
                        s = q * 4 + t
                        P.do("tensor", lambda e, t=t, s=s,
                             b=brb[nd % 2]: e.matmul(
                                 p_a[:, t * 512:(t + 1) * 512], rows_o(s),
                                 b, start=True, stop=False))
                for jj in range(8):
                    jc = grp * 8 + jj
                    for t in range(4):
                        fn = lambda e, jc=jc, t=t, gbp=gbp, jj=jj: e.matmul(
                            p_a[:, t * 512:(t + 1) * 512],
                            hit[:, jc * 512 + t * 128:jc * 512 + t * 128 + 128],
                            owqb[gbp][:, jj * 512:(jj + 1) * 512],
                            start=False, stop=(jc == NJC - 1))
                        if jj == 7 and t == 3:
                            pe_o = P.inc("tensor", "PE", fn)
                        else:
                            P.do("tensor", fn)
                lb_hist[key] = pe_o
            for t in range(4):
                s = q * 4 + t
                if dmg == 0:
                    if q > 0:
                        P.w("sync", "DV", dv_zdone[(q - 1, t)])
                    if t >= 2:
                        P.w("sync", "LX%d" % (t % 2), lx_hist[(q, t - 2)])
                    lx = P.inc("sync", "LX%d" % (t % 2), lambda e, s=s, t=t:
                               e.dma_start(x1r[t], x1_scr[s, :, :]), 16)
                    lx_hist[(q, t)] = lx
                P.w("vector", "PE", lb_hist[(q, dmg, 7)])
                if dmg == 0:
                    P.w("vector", "LX%d" % (t % 2), lx_hist[(q, t)])
                    if q == 0 and t == 0:
                        P.w("vector", "SX0", P.val("SX0"))
                        P.w("vector", "SX1", P.val("SX1"))
                    if q > 0:
                        so_prev = st_out[(q - 1) * 4 + t]
                        P.w("vector", "SO%d" % (((q - 1) * 4 + t) % 2),
                            so_prev)
                s2c = ccol(C_S2, s)
                P.do("vector", lambda e: e.drain())
                dv_z = P.inc("vector", "DV",
                             lambda e, t=t, dmg=dmg, s2c=s2c:
                             e.scalar_tensor_tensor(
                                 zpart[t][:, dmg * 512:(dmg + 1) * 512],
                                 p_a[:, t * 512:(t + 1) * 512], s2c,
                                 x1r[t][:, dmg * 512:(dmg + 1) * 512],
                                 ALU.mult, ALU.add))
                dv_zmm[(q, dmg, t)] = dv_z
        for t in range(4):
            s = q * 4 + t
            P.w("vector", "DV", dv_zmm[(q, 3, t)])
            zt = zpart[t]
            mcol, vcol, rcol = col(C_T0), col(C_T1), col(C_T3)
            P.do("vector", lambda e: e.drain())
            P.do("vector", lambda e, zt=zt: e.tensor_reduce(mcol, zt, AX.X,
                                                            ALU.add))
            P.do("vector", lambda e: e.drain())
            P.do("vector", lambda e: e.tensor_single_scalar(mcol, mcol, INV_D,
                                                            ALU.mult))
            P.do("vector", lambda e: e.drain())
            dv_l2 = P.inc("vector", "DV", lambda e, zt=zt:
                          e.tensor_single_scalar(zt, zt, mcol, ALU.subtract))
            P.w("scalar", "DV", dv_l2)
            P.do("scalar", lambda e, zt=zt: e.activation(sgt, zt, AF.Square,
                                                         accum_out=vcol))
            P.do("scalar", lambda e: e.drain())
            P.do("scalar", lambda e: e.activation(col(C_T2), vcol, AF.Ln,
                                                  bias=col(C_EPS),
                                                  scale=INV_D))
            P.do("scalar", lambda e: e.drain())
            ac_l2 = P.inc("scalar", "AC", lambda e: e.activation(
                rcol, col(C_T2), AF.Exp, scale=-0.5))
            P.w("vector", "AC", ac_l2)
            bp3 = s % 2
            if s >= 2:
                P.w("vector", "SO%d" % bp3, st_out[s - 2])
            P.do("vector", lambda e: e.drain())
            dv_o = P.inc("vector", "DV", lambda e, zt=zt:
                         e.tensor_single_scalar(zt, zt, rcol, ALU.mult))
            dv_zdone[(q, t)] = dv_o
            P.w("sync", "DV", dv_o)
            st_out[s] = P.inc("sync", "SO%d" % bp3, lambda e, s=s, zt=zt:
                              e.dma_start(out_h[s * 128:(s + 1) * 128, :], zt),
                              16)
        pe_q_mm[q] = P.val("PE")

    P.w("sync", "SO0", P.val("SO0"))
    P.w("sync", "SO1", P.val("SO1"))

    with nc.Block() as block:
        @block.sync
        def _(eng):
            for f in P.ops["sync"]:
                f(eng)

        @block.tensor
        def _(eng):
            for f in P.ops["tensor"]:
                f(eng)

        @block.vector
        def _(eng):
            for f in P.ops["vector"]:
                f(eng)

        @block.scalar
        def _(eng):
            for f in P.ops["scalar"]:
                f(eng)

        @block.gpsimd
        def _(eng):
            for f in P.ops["gpsimd"]:
                f(eng)

    ctx.close()
    return nc


_NC_CACHE = None


def kernel(L0_attn_u, L0_attn_A_factors, L0_attn_W_b_rank, L0_attn_W_b_state,
           L0_attn_W_c_rank, L0_attn_W_c_model, L0_attn_D_vec,
           L0_ffn_gate_w, L0_ffn_gate_b, L0_ffn_val_w, L0_ffn_val_b,
           L0_ffn_out_w, L0_ffn_out_b):
    global _NC_CACHE
    u = np.ascontiguousarray(np.asarray(L0_attn_u, dtype=np.float32).reshape(-1, DM))
    gw = np.asarray(L0_ffn_gate_w, dtype=np.float32)
    vw = np.asarray(L0_ffn_val_w, dtype=np.float32)
    ow = np.asarray(L0_ffn_out_w, dtype=np.float32)
    shared = dict(
        gwt=np.ascontiguousarray(gw.T),
        vwt=np.ascontiguousarray(vw.T),
        owt=np.ascontiguousarray(ow.T),
        gbr=np.ascontiguousarray(np.asarray(L0_ffn_gate_b,
                                            dtype=np.float32).reshape(1, HID)),
        vbr=np.ascontiguousarray(np.asarray(L0_ffn_val_b,
                                            dtype=np.float32).reshape(1, HID)),
        obr=np.ascontiguousarray(np.asarray(L0_ffn_out_b,
                                            dtype=np.float32).reshape(1, DM)),
        wbr=np.ascontiguousarray(np.asarray(L0_attn_W_b_rank, dtype=np.float32)),
        wbs=np.ascontiguousarray(np.asarray(L0_attn_W_b_state, dtype=np.float32)),
        wcr=np.ascontiguousarray(np.asarray(L0_attn_W_c_rank, dtype=np.float32)),
        wcm=np.ascontiguousarray(np.asarray(L0_attn_W_c_model, dtype=np.float32)),
        afac=np.ascontiguousarray(
            np.asarray(L0_attn_A_factors, dtype=np.float32).reshape(1, 32)),
        dvec=np.ascontiguousarray(
            np.asarray(L0_attn_D_vec, dtype=np.float32).reshape(1, DM)),
        idf=np.eye(128, dtype=np.float32),
        idb=np.eye(128, dtype=np.float32).astype(BF),
    )
    if _NC_CACHE is None:
        _NC_CACHE = build_program()
    in_maps = []
    for c in range(NCORE):
        uc = np.ascontiguousarray(u[c * TOK:(c + 1) * TOK, :])
        in_maps.append(dict(u=uc, ut=np.ascontiguousarray(uc.T), **shared))
    res = run_bass_kernel_spmd(_NC_CACHE, in_maps, core_ids=list(range(NCORE)))
    out = np.concatenate(
        [np.asarray(r["out"], dtype=np.float32) for r in res.results], axis=0)
    return out.reshape(4, 4096, DM)
